# version 02 (final)
# baseline (speedup 1.0000x reference)
import numpy as np
import ml_dtypes
import concourse.bass as bass
import concourse.mybir as mybir
from concourse.bass_utils import run_bass_kernel_spmd

F32 = mybir.dt.float32
BF16 = mybir.dt.bfloat16
I32 = mybir.dt.int32
AF = mybir.ActivationFunctionType
ALU = mybir.AluOpType
AX = mybir.AxisListType
BF = ml_dtypes.bfloat16

T = 4096
D = 1024
NT = T // 128
NEG = -30000.0

LAUNCH_PLAN = [[0], [1], [2], [3]]


class Prog:
    COMPUTE = ("pe", "act", "dve", "pool", "sp")
    QUEUES = ("sp", "act", "pool")

    def __init__(self, nc, n_dma_sems=10):
        self.nc = nc
        self.steps = []
        self.esem = {e: nc.alloc_semaphore(f"es_{e}") for e in self.COMPUTE}
        self.qsem = {q: [nc.alloc_semaphore(f"qs_{q}_{i}") for i in range(n_dma_sems)]
                     for q in self.QUEUES}
        self.quse = {q: [0] * n_dma_sems for q in self.QUEUES}
        self.qrr = {q: 0 for q in self.QUEUES}
        self.cur_barrier = None
        self.since_barrier = []

    def add(self, engine, fn, deps=(), nobarrier=False):
        dl = [d for d in deps if d is not None]
        if not nobarrier and self.cur_barrier is not None:
            dl.append(self.cur_barrier)
        rec = dict(kind="c", engine=engine, fn=fn, deps=dl,
                   idx=len(self.steps), needed=False, token=None)
        self.steps.append(rec)
        self.since_barrier.append(rec)
        return rec

    def dma(self, queue, out, in_, deps=(), nobarrier=False):
        dl = [d for d in deps if d is not None]
        if not nobarrier and self.cur_barrier is not None:
            dl.append(self.cur_barrier)
        rec = dict(kind="d", engine=queue, queue=queue, out=out, in_=in_,
                   deps=dl, idx=len(self.steps), needed=False, token=None)
        self.steps.append(rec)
        self.since_barrier.append(rec)
        return rec

    def barrier(self):
        deps = list(self.since_barrier)
        rec = self.add("sp", lambda e: e.nop(), deps=deps, nobarrier=True)
        self.cur_barrier = rec
        self.since_barrier = [rec]
        return rec

    def finalize(self):
        nc = self.nc
        self.barrier()
        for s in self.steps:
            for d in s["deps"]:
                d["needed"] = True
        ecount = {e: 0 for e in self.COMPUTE}
        for s in self.steps:
            if s["kind"] == "d":
                q = s["queue"]
                i = self.qrr[q]
                self.qrr[q] = (i + 1) % len(self.qsem[q])
                s["slot"] = i
                s["prev_uses"] = self.quse[q][i]
                self.quse[q][i] += 1
                s["token"] = (self.qsem[q][i], 16 * self.quse[q][i])
            elif s["needed"]:
                ecount[s["engine"]] += 1
                s["token"] = (self.esem[s["engine"]], ecount[s["engine"]])

        per_engine = {e: [s for s in self.steps if s["engine"] == e]
                      for e in self.COMPUTE}

        def emit(eng, name):
            for s in per_engine[name]:
                waits = {}
                for d in s["deps"]:
                    if (d["kind"] == "c" and d["engine"] == name
                            and d["idx"] < s["idx"] and name == "pe"):
                        continue
                    sem, val = d["token"]
                    waits[sem.num] = (sem, max(val, waits.get(sem.num, (sem, 0))[1]))
                for sem, val in waits.values():
                    eng.wait_ge(sem, val)
                if s["kind"] == "d":
                    if s["prev_uses"] > 0:
                        eng.wait_ge(self.qsem[s["queue"]][s["slot"]],
                                    16 * s["prev_uses"])
                    eng.dma_start(out=s["out"], in_=s["in_"]).then_inc(
                        self.qsem[s["queue"]][s["slot"]], 16)
                else:
                    last = s["fn"](eng)
                    if s["needed"]:
                        assert last is not None, f"step {s['idx']}: no last inst"
                        last.then_inc(self.esem[name], 1)

        with nc.Block() as blk:
            blk.tensor(lambda e: emit(e, "pe"))
            blk.scalar(lambda e: emit(e, "act"))
            blk.vector(lambda e: emit(e, "dve"))
            blk.gpsimd(lambda e: emit(e, "pool"))
            blk.sync(lambda e: emit(e, "sp"))


class Ctx:
    def __init__(self, nc, consts):
        self.nc = nc
        self.p = Prog(nc)
        ap = lambda name, shape, dt: nc.alloc_sbuf_tensor(name, shape, dt).ap()
        self.XNFM = ap("XNFM", [128, 8 * T], BF16)
        self.FMB = ap("FMB", [128, 8 * T], BF16)
        self.FMC = ap("FMC", [128, 4 * T], BF16)
        self.SMB = ap("SMB", [128, 6 * 1024], BF16)
        self.WS = ap("WS", [128, 2 * 1024], BF16)
        self.SCR = ap("SCR", [128, 2048], F32)
        self.STAT = ap("STAT", [128, 512], F32)
        MSK = ap("MSK", [128, T], BF16)
        self.MSKQ = MSK[0:16, :]
        self.MSKK = MSK[32:48, :]
        self.MSKV = MSK[64:80, :]
        self.MSKT = ap("MSKT", [128, 3 * 512], BF16)
        self.B2B = ap("B2B", [128, 1024], BF16)
        self.PS = [nc.alloc_psum_tensor(f"ps{i}", [128, 512], F32).ap()
                   for i in range(8)]
        self.PSB = [x.bitcast(BF16) for x in self.PS]
        self.bank_w = [None] * 8
        self.bank_r = [None] * 8
        self.c = {}
        self.craw = {}
        for name, arr in consts.items():
            dt = {np.dtype(np.float32): F32, np.dtype(BF): BF16}[arr.dtype]
            h = nc.dram_tensor("c_" + name, list(arr.shape), dt,
                               kind="ExternalInput").ap()
            if name.startswith("rot_"):
                self.craw[name] = h
                continue
            s = ap("sc_" + name, list(arr.shape), dt)
            self.p.dma("sp", s, h)
            self.c[name] = s
        for i, v in enumerate(CONST_VALS):
            nc.const_aps.register(F32, v, self.c["cvals"][:, i:i + 1])
        self.p.barrier()

    def mm(self, fn, banks, deps=()):
        dl = list(deps) + [self.bank_r[b] for b in banks]
        st = self.p.add("pe", fn, deps=dl)
        for b in banks:
            self.bank_w[b] = st
        return st

    def ev(self, eng, fn, banks, deps=()):
        dl = list(deps) + [self.bank_w[b] for b in banks]
        st = self.p.add(eng, fn, deps=dl)
        for b in banks:
            self.bank_r[b] = st
        return st


def fmcols(c, t0, n):
    return slice(c * T + t0, c * T + t0 + n)


def load_w_nat(ctx, w, dst, dst_off, queue="pool", deps=()):
    I, O = w.shape
    nchunk = (I + 127) // 128
    toks = []
    for c in range(nchunk):
        r = min(128, I - c * 128)
        toks.append(ctx.p.dma(queue,
                              dst[0:r, dst_off + c * O:dst_off + (c + 1) * O],
                              w[c * 128:c * 128 + r, :], deps=deps))

    def lhsT(c, e0, ew):
        r = min(128, I - c * 128)
        return dst[0:r, dst_off + c * O + e0:dst_off + c * O + e0 + ew]
    return toks, lhsT


def load_wT(ctx, w, dst, dst_off, queue="pool", deps=(), scale=None,
            stage=None, banks=(6, 7)):
    p = ctx.p
    O, I = w.shape
    nco = (O + 127) // 128
    nci = (I + 127) // 128
    is_int = (w.dtype == I32)
    toks = []
    prev_stage = [None, None]
    prev_int = [None]
    for co in range(nco):
        r = min(128, O - co * 128)
        sp_ = co % 2
        if stage is None:
            stg = ctx.SMB[0:r, sp_ * I:sp_ * I + I]
        else:
            stg = stage[0:r, sp_ * I:sp_ * I + I]
        if is_int:
            dd = prev_stage[sp_]
            for s0 in range(0, I, 2048):
                sw = min(2048, I - s0)
                stgi = ctx.SCR.bitcast(I32)[0:r, 0:sw]
                d1 = p.dma(queue, stgi, w[co * 128:co * 128 + r, s0:s0 + sw],
                           deps=[dd, prev_int[0]])
                dd = p.add("dve", lambda v, o=stg[0:r, s0:s0 + sw], i=stgi,
                           s=scale: v.tensor_scalar_mul(o, i, s), deps=[d1])
                prev_int[0] = dd
        else:
            dd = p.dma(queue, stg, w[co * 128:co * 128 + r, :],
                       deps=[prev_stage[sp_]])
            if scale is not None:
                dd = p.add("dve", lambda v, o=stg, s=scale:
                           v.tensor_scalar_mul(o, o, s), deps=[dd])
        last_cp = None
        for ci0 in range(0, nci, 4):
            grp = min(4, nci - ci0)
            bank = banks[(ci0 // 4) % 2]

            def trf(te, stg=stg, ci0=ci0, grp=grp, bank=bank, r=r):
                last = None
                for j in range(grp):
                    iw = min(128, I - (ci0 + j) * 128)
                    last = te.transpose(
                        ctx.PSB[bank][0:iw, j * 128:j * 128 + r],
                        stg[0:r, (ci0 + j) * 128:(ci0 + j) * 128 + iw],
                        ctx.c["ident_bf"][0:r, 0:r])
                return last
            tr = ctx.mm(trf, [bank], deps=[dd])

            def cpf(v, ci0=ci0, grp=grp, bank=bank, co=co, r=r):
                last = None
                for j in range(grp):
                    iw = min(128, I - (ci0 + j) * 128)
                    last = v.tensor_copy(
                        dst[0:iw, dst_off + (ci0 + j) * O + co * 128:
                            dst_off + (ci0 + j) * O + co * 128 + r],
                        ctx.PSB[bank][0:iw, j * 128:j * 128 + r])
                return last
            last_cp = ctx.ev("dve", cpf, [bank], deps=[tr])
            toks.append(last_cp)
        prev_stage[sp_] = last_cp

    def lhsT(ci, o0, ow):
        r = min(128, I - ci * 128)
        return dst[0:r, dst_off + ci * O + o0:dst_off + ci * O + o0 + ow]
    return toks, lhsT


def bcast_row(ctx, row, width, dst, deps=(), bank=6):
    toks = []
    for n0 in range(0, width, 512):
        nw = min(512, width - n0)

        def f(te, n0=n0, nw=nw):
            return te.matmul(ctx.PS[bank][:, 0:nw], ctx.c["ones1_f32"],
                             row[:, n0:n0 + nw], start=True, stop=True)
        mm = ctx.mm(f, [bank], deps=deps)
        toks.append(ctx.ev("dve", lambda v, n0=n0, nw=nw:
                           v.tensor_copy(dst[:, n0:n0 + nw],
                                         ctx.PS[bank][:, 0:nw]),
                           [bank], deps=[mm]))
    return toks


def tm_to_fm(ctx, src_tm, dst_fm_writer, t, nch, deps=(), banks=(6, 7),
             eng="dve"):
    last = None
    for h0 in range(0, nch, 4):
        grp = min(4, nch - h0)
        bank = banks[(h0 // 4) % 2]

        def trf(te, h0=h0, grp=grp, bank=bank):
            l = None
            for j in range(grp):
                l = te.transpose(ctx.PSB[bank][:, j * 128:(j + 1) * 128],
                                 src_tm[:, (h0 + j) * 128:(h0 + j + 1) * 128],
                                 ctx.c["ident_bf"])
            return l
        tr = ctx.mm(trf, [bank], deps=list(deps) + [last])

        def cpf(v, h0=h0, grp=grp, bank=bank):
            l = None
            for j in range(grp):
                l = dst_fm_writer(v, h0 + j,
                                  ctx.PSB[bank][:, j * 128:(j + 1) * 128])
            return l
        last = ctx.ev(eng, cpf, [bank], deps=[tr])
    return last


def emit_ln(ctx, x_hbm, dst_fm=None, stat_dump=None, tm_dump=None,
            f32_dump=None):
    p = ctx.p
    p.barrier()
    dst = dst_fm if dst_fm is not None else ctx.XNFM
    prev = {0: None, 1: None, 2: None, 3: None}
    prev_sq = {0: None, 1: None, 2: None, 3: None}
    prev_sd = {0: None, 1: None}
    for t in range(NT):
        par = t % 2
        par4 = t % 4
        xt = ctx.SCR[:, par * 1024:par * 1024 + D]
        d_x = p.dma("sp", xt, x_hbm[t * 128:(t + 1) * 128, :],
                    deps=[prev[par], prev_sq[par]])
        sq = ctx.STAT[:, t * 4:t * 4 + 1]
        sm = ctx.STAT[:, t * 4 + 1:t * 4 + 2]
        rst = ctx.STAT[:, t * 4 + 2:t * 4 + 3]
        mean = ctx.STAT[:, t * 4 + 3:t * 4 + 4]
        sqo = ctx.SMB[:, 4096 + par * 1024:4096 + par * 1024 + D]
        a1 = p.add("act", lambda s, i=xt, o=sqo:
                   s.activation(o, i, AF.Square), deps=[d_x])
        prev_sq[par] = a1
        a1 = p.add("dve", lambda v, i=sqo, o=sq:
                   v.tensor_reduce(o, i, AX.X, ALU.add), deps=[a1])
        a2 = p.add("dve", lambda v, i=xt, o=sm:
                   v.tensor_reduce(o, i, AX.X, ALU.add), deps=[d_x])
        a3 = p.add("dve", lambda v, m=mean, s=sm:
                   v.tensor_scalar_mul(m, s, 1.0 / D), deps=[a2])
        a4 = p.add("dve", lambda v, m=mean, r=rst:
                   v.tensor_mul(r, m, m), deps=[a3])
        a5 = p.add("dve", lambda v, q=sq, r=rst:
                   v.scalar_tensor_tensor(r, q, 1.0 / D, r,
                                          ALU.mult, ALU.subtract),
                   deps=[a4, a1])
        a6 = p.add("act", lambda s, r=rst:
                   s.activation(r, r, AF.Ln, bias=1e-5), deps=[a5])
        a7 = p.add("act", lambda s, r=rst:
                   s.activation(r, r, AF.Exp, scale=-0.5), deps=[a6])
        xnb = ctx.SMB[:, par * 1024:par * 1024 + D]
        a8 = p.add("dve", lambda v, o=xnb, i=xt, m=mean, r=rst:
                   v.tensor_scalar(o, i, m, r, ALU.subtract, ALU.mult),
                   deps=[a7, a3, d_x])

        def wr(v, c, psb, t=t):
            return v.tensor_copy(dst[:, fmcols(c, t * 128, 128)], psb)
        prev[par] = tm_to_fm(ctx, xnb, wr, t, 8, deps=[a8])
        if f32_dump is not None:
            a8f = p.add("dve", lambda v, i=xt, m=mean, r=rst:
                        v.tensor_scalar(i, i, m, r, ALU.subtract, ALU.mult),
                        deps=[a8, prev_sq[par], prev[par]])
            prev[par] = p.dma("sp", f32_dump[t * 128:(t + 1) * 128, :], xt,
                              deps=[a8f])
        if tm_dump is not None:
            prev[par] = ctx.p.dma("pool", tm_dump[t * 128:(t + 1) * 128, :],
                                  xnb, deps=[a8, prev[par]])
        if stat_dump is not None:
            sd = ctx.STAT[:, 200 + par * 8:200 + par * 8 + 4]
            c1 = p.add("dve", lambda v, sd=sd, a=sq, b=sm, c=rst, dd=mean:
                       [v.tensor_copy(sd[:, 0:1], a),
                        v.tensor_copy(sd[:, 1:2], b),
                        v.tensor_copy(sd[:, 2:3], c),
                        v.tensor_copy(sd[:, 3:4], dd)][-1],
                       deps=[a8, prev[par], prev_sd[par]])
            prev_sd[par] = p.dma("sp", stat_dump[t * 128:(t + 1) * 128, 0:4],
                                 sd, deps=[c1])
    return p.barrier()


def l0_attn(ctx, W, x_in, xa):
    p = ctx.p
    p.barrier()
    tq, wq = load_wT(ctx, W["L0_attn_Wq"], ctx.FMB, 0,
                     stage=ctx.SMB[:, 0:2048])
    tk, wk = load_wT(ctx, W["L0_attn_Wk"], ctx.FMB, 8 * 1024,
                     stage=ctx.SMB[:, 2048:4096])
    tv, wv = load_wT(ctx, W["L0_attn_Wv"], ctx.FMB, 16 * 1024,
                     stage=ctx.SMB[:, 4096:6144])
    tp, wp = load_w_nat(ctx, W["L0_attn_proj"], ctx.FMB, 24 * 1024)
    p.barrier()

    qp_fm = ctx.FMC
    last_kv = None
    for g in range(8):
        t0 = g * 512
        for (dstb, off, wacc) in ((ctx.SMB, 0, wq), (ctx.FMC, 2 * T, wk)):
            for e in range(8):
                bank = 4 + (e % 2)

                def f(te, e=e, t0=t0, wacc=wacc, bank=bank):
                    last = None
                    for c in range(8):
                        last = te.matmul(ctx.PS[bank][:, 0:512],
                                         wacc(c, e * 128, 128),
                                         ctx.XNFM[:, fmcols(c, t0, 512)],
                                         start=(c == 0), stop=(c == 7))
                    return last
                mmv = ctx.mm(f, [bank])
                ctx.ev("dve", lambda v, e=e, dstb=dstb, off=off, bank=bank:
                       v.tensor_copy(dstb[:, off + e * 512:off + (e + 1) * 512],
                                     ctx.PS[bank][:, 0:512]),
                       [bank], deps=[mmv])
        for (srcb, src_off, is_q) in ((ctx.SMB, 0, True), (ctx.FMC, 2 * T, False)):
            for pc in range(2):
                bank = 6 + pc

                def f2(te, srcb=srcb, src_off=src_off, pc=pc, bank=bank):
                    last = None
                    for c in range(8):
                        last = te.matmul(
                            ctx.PS[bank][:, 0:512], wp(c, pc * 128, 128),
                            srcb[:, src_off + c * 512:src_off + (c + 1) * 512],
                            start=(c == 0), stop=(c == 7))
                    return last
                mmv = ctx.mm(f2, [bank])
                if is_q:
                    dstp = qp_fm[:, fmcols(pc, t0, 512)]
                else:
                    dstp = ctx.SMB[:, 4096 + pc * 512:4096 + (pc + 1) * 512]
                sc1 = ctx.SCR[:, pc * 512:(pc + 1) * 512]
                e1 = ctx.ev("dve", lambda v, bank=bank, sc1=sc1:
                            v.tensor_single_scalar(sc1, ctx.PS[bank][:, 0:512],
                                                   0.0, ALU.min),
                            [bank], deps=[mmv])
                e2 = p.add("act", lambda s, sc1=sc1:
                           s.activation(sc1, sc1, AF.Exp), deps=[e1])
                e3 = ctx.ev("dve", lambda v, bank=bank, dstp=dstp, sc1=sc1:
                            v.scalar_tensor_tensor(dstp, ctx.PS[bank][:, 0:512],
                                                   0.0, sc1, ALU.max, ALU.add),
                            [bank], deps=[e2, mmv])
        for tt in range(4):
            t = g * 4 + tt

            def fv(te, t=t):
                last = None
                for n in range(2):
                    for c in range(8):
                        last = te.matmul(ctx.PS[4 + n][:, 0:512],
                                         ctx.XNFM[:, fmcols(c, t * 128, 128)],
                                         wv(c, n * 512, 512),
                                         start=(c == 0), stop=(c == 7))
                return last
            mmv = ctx.mm(fv, [4, 5])
            vtile = ctx.SMB[:, 5120:5120 + 1024]
            ev = ctx.ev("dve", lambda v, vtile=vtile:
                        [v.tensor_copy(vtile[:, 0:512], ctx.PS[4][:, 0:512]),
                         v.tensor_copy(vtile[:, 512:1024], ctx.PS[5][:, 0:512])][-1],
                        [4, 5], deps=[mmv])
            kp_t = ctx.SCR.bitcast(BF16)[:, 3840:3840 + 256]

            def ftr(te, tt=tt):
                last = None
                for pc in range(2):
                    last = te.transpose(
                        ctx.PSB[6][:, pc * 128:(pc + 1) * 128],
                        ctx.SMB[:, 4096 + pc * 512 + tt * 128:
                                4096 + pc * 512 + tt * 128 + 128],
                        ctx.c["ident_bf"])
                return last
            trk = ctx.mm(ftr, [6])
            cpk = ctx.ev("dve", lambda v, kp_t=kp_t:
                         v.tensor_copy(kp_t, ctx.PSB[6][:, 0:256]),
                         [6], deps=[trk])

            def fkv(te, kp_t=kp_t, vtile=vtile, t=t):
                last = None
                for pc in range(2):
                    for n in range(2):
                        last = te.matmul(ctx.PS[pc * 2 + n][:, 0:512],
                                         kp_t[:, pc * 128:(pc + 1) * 128],
                                         vtile[:, n * 512:(n + 1) * 512],
                                         start=(t == 0), stop=(t == NT - 1))
                return last
            last_kv = ctx.mm(fkv, [0, 1, 2, 3], deps=[cpk, ev])
    z1 = p.add("dve", lambda v: v.memset(ctx.SMB[:, 0:2304], 0.0),
               deps=[last_kv])
    kvt = []
    for pc in range(2):
        for n in range(2):
            kvt.append(ctx.ev("dve", lambda v, pc=pc, n=n:
                              v.tensor_copy(
                                  ctx.SMB[:, pc * 1152 + n * 512:
                                          pc * 1152 + (n + 1) * 512],
                                  ctx.PS[pc * 2 + n][:, 0:512]),
                              [pc * 2 + n], deps=[z1]))
    o1 = p.add("dve", lambda v: v.memset(ctx.SMB[:, 1024:1025], 1.0),
               deps=kvt)
    o2 = p.add("dve", lambda v: v.memset(ctx.SMB[:, 1152 + 1024:1152 + 1025], 1.0),
               deps=[o1])
    p.barrier()

    prevst = {0: None, 1: None}
    for t in range(NT):
        par = t % 2
        bset = [par * 3, par * 3 + 1, par * 3 + 2]

        def fb(te, t=t, bset=bset):
            last = None
            for pc in range(2):
                for n in range(3):
                    nw = 512 if n < 2 else 128
                    last = te.matmul(
                        ctx.PS[bset[n]][:, 0:nw],
                        qp_fm[:, fmcols(pc, t * 128, 128)],
                        ctx.SMB[:, pc * 1152 + n * 512:pc * 1152 + n * 512 + nw],
                        start=(pc == 0), stop=(pc == 1))
            return last
        mmv = ctx.mm(fb, bset)
        zc = ctx.STAT[:, 16 + par:17 + par]
        e1 = ctx.ev("dve", lambda v, zc=zc, b=bset[2]:
                    v.tensor_scalar_add(zc, ctx.PS[b][:, 0:1], 1e-8),
                    [bset[2]], deps=[mmv])
        e2 = p.add("dve", lambda v, zc=zc: v.reciprocal(zc, zc), deps=[e1])
        xt = ctx.SCR[:, par * 1024:par * 1024 + 1024]
        d_x = p.dma("sp", xt, x_in[t * 128:(t + 1) * 128, :],
                    deps=[prevst[par]])
        ev = [ctx.ev("dve", lambda v, n=n, xt=xt, zc=zc, b=bset[n]:
                     v.scalar_tensor_tensor(
                         xt[:, n * 512:(n + 1) * 512], ctx.PS[b][:, 0:512],
                         zc, xt[:, n * 512:(n + 1) * 512], ALU.mult, ALU.add),
                     [bset[n]], deps=[e2, d_x]) for n in range(2)]
        st = p.dma("sp", xa[t * 128:(t + 1) * 128, :], xt, deps=ev)
        prevst[par] = st
    p.barrier()


def fff_routing(ctx, W, pfx, msk_fm, mskt_off):
    p = ctx.p
    p.barrier()
    tw, nwT = load_wT(ctx, W[f"L0_ffn_{pfx}_node_w"], ctx.WS, 0, queue="pool")
    nb_row = ctx.STAT[0:1, 32:47]
    d_b = p.dma("sp", nb_row, W[f"L0_ffn_{pfx}_node_b"].unsqueeze(0))
    nb_bc = ctx.STAT[:, 48:63]
    bcast_row(ctx, nb_row, 15, nb_bc, deps=[d_b] + tw, bank=6)
    p.barrier()
    for t in range(NT):
        bank = t % 2

        def f2(te, t=t, bank=bank):
            last = None
            for c in range(8):
                last = te.matmul(ctx.PS[bank][:, 0:15],
                                 ctx.XNFM[:, fmcols(c, t * 128, 128)],
                                 nwT(c, 0, 15),
                                 start=(c == 0), stop=(c == 7))
            return last
        mmv = ctx.mm(f2, [bank])
        sc = ctx.STAT[:, 64 + bank * 32:64 + bank * 32 + 15]
        e1 = ctx.ev("dve", lambda v, sc=sc, bank=bank:
                    v.tensor_add(sc, ctx.PS[bank][:, 0:15], nb_bc),
                    [bank], deps=[mmv])
        wsl = ctx.STAT[:, 128 + bank * 16:128 + bank * 16 + 16]
        e2 = p.add("dve", lambda v, sc=sc, w=wsl: fff_tree_step(v, sc, w),
                   deps=[e1])
        mskf = ctx.STAT[:, 160 + bank * 16:160 + bank * 16 + 16]
        e3 = p.add("dve", lambda v, w=wsl, m=mskf: fff_onehot(v, w, m, ctx),
                   deps=[e2])
        e4 = p.add("dve", lambda v, m=mskf, t=t:
                   v.tensor_copy(ctx.MSKT[:, mskt_off + t * 16:
                                          mskt_off + (t + 1) * 16], m),
                   deps=[e3])
        tr = ctx.mm(lambda te, t=t, bank=bank: te.transpose(
            ctx.PSB[6 + bank][0:16, 0:128],
            ctx.MSKT[:, mskt_off + t * 16:mskt_off + (t + 1) * 16],
            ctx.c["ident_bf"][0:128, 0:128]),
            [6 + bank], deps=[e4])
        ctx.ev("dve", lambda v, t=t, bank=bank:
               v.tensor_copy(msk_fm[:, t * 128:(t + 1) * 128],
                             ctx.PSB[6 + bank][0:16, 0:128]),
               [6 + bank], deps=[tr])
    return p.barrier()


def fff_tree_step(v, sc, w):
    b = [w[:, i:i + 1] for i in range(4)]
    s1, t1, t2 = w[:, 4:5], w[:, 5:6], w[:, 6:7]
    u = [w[:, 7 + i:8 + i] for i in range(6)]

    def sel(out, hi, lo, bit):
        v.tensor_sub(out, hi, lo)
        v.scalar_tensor_tensor(out, out, bit, lo, ALU.mult, ALU.add)

    v.tensor_single_scalar(b[0], sc[:, 0:1], 0.0, ALU.is_ge)
    sel(s1, sc[:, 2:3], sc[:, 1:2], b[0])
    v.tensor_single_scalar(b[1], s1, 0.0, ALU.is_ge)
    sel(t1, sc[:, 4:5], sc[:, 3:4], b[1])
    sel(t2, sc[:, 6:7], sc[:, 5:6], b[1])
    sel(s1, t2, t1, b[0])
    v.tensor_single_scalar(b[2], s1, 0.0, ALU.is_ge)
    sel(u[0], sc[:, 8:9], sc[:, 7:8], b[2])
    sel(u[1], sc[:, 10:11], sc[:, 9:10], b[2])
    sel(u[2], sc[:, 12:13], sc[:, 11:12], b[2])
    sel(u[3], sc[:, 14:15], sc[:, 13:14], b[2])
    sel(u[4], u[1], u[0], b[1])
    sel(u[5], u[3], u[2], b[1])
    sel(s1, u[5], u[4], b[0])
    return v.tensor_single_scalar(b[3], s1, 0.0, ALU.is_ge)


def fff_onehot(v, w, mskf, ctx):
    leaf = w[:, 14:15]
    v.tensor_scalar_mul(leaf, w[:, 0:1], 8.0)
    v.scalar_tensor_tensor(leaf, w[:, 1:2], 4.0, leaf, ALU.mult, ALU.add)
    v.scalar_tensor_tensor(leaf, w[:, 2:3], 2.0, leaf, ALU.mult, ALU.add)
    v.tensor_add(leaf, leaf, w[:, 3:4])
    return v.tensor_tensor(mskf, ctx.c["iota16_bc"],
                           leaf.broadcast_to([128, 16]), ALU.is_equal)


def fff_apply(ctx, W, pfx, msk_fm, mskt_off, out_mode, out):
    p = ctx.p
    p.barrier()
    w1 = W[f"L0_ffn_{pfx}_w1"]
    w2 = W[f"L0_ffn_{pfx}_w2"]
    b1 = W[f"L0_ffn_{pfx}_b1"]
    b2 = W[f"L0_ffn_{pfx}_b2"]
    toks = []
    for lg in range(2):
        for c in range(8):
            toks.append(p.dma(
                "pool",
                ctx.FMC.rearrange("p (a l h) -> p a l h", a=32, l=8)[:, lg * 8 + c],
                w1.rearrange("l (c p) h -> p c l h", p=128)[:, c, lg * 8:(lg + 1) * 8, :]))
    w2off = 2 * T
    for lg in range(2):
        for c2 in range(4):
            toks.append(p.dma(
                "pool",
                ctx.FMC[:, w2off + lg * 4096 + c2 * 1024:
                        w2off + lg * 4096 + (c2 + 1) * 1024],
                w2.rearrange("(g l) h d -> g (l h) d", l=2)[lg * 4 + c2]))
    b1r = ctx.SMB[0:1, 1024:1024 + 1024]
    toks.append(p.dma("pool", b1r, b1.rearrange("l h -> (l h)").unsqueeze(0)))
    mb = msk_fm.base_partition()
    toks.append(p.dma("pool", ctx.B2B[mb:mb + 16, :], b2))
    p.barrier()

    prev_st = {0: None, 1: None}
    prev_hsc = {0: None, 1: None}
    for t in range(NT):
        par = t % 2
        hb = []
        for lg in range(2):
            bank = lg

            def fh(te, t=t, lg=lg, bank=bank):
                last = None
                for c in range(8):
                    last = te.matmul(
                        ctx.PS[bank][:, 0:512],
                        ctx.XNFM[:, fmcols(c, t * 128, 128)],
                        ctx.FMC[:, lg * 4096 + c * 512:lg * 4096 + (c + 1) * 512],
                        start=(c == 0), stop=False)
                return te.matmul(ctx.PS[bank][:, 0:512], ctx.c["ones1_bf"],
                                 b1r[:, lg * 512:(lg + 1) * 512],
                                 start=False, stop=True)
            mmv = ctx.mm(fh, [bank])
            hsc = ctx.SCR[:, lg * 512:lg * 512 + 512]
            e1 = ctx.ev("act", lambda s, hsc=hsc, bank=bank:
                        s.activation(hsc, ctx.PS[bank][:, 0:512], AF.Relu),
                        [bank], deps=[mmv, prev_hsc[lg]])
            hdst = ctx.SMB[:, lg * 512:lg * 512 + 512]
            mt = ctx.MSKT[:, mskt_off + t * 16 + lg * 8:
                          mskt_off + t * 16 + (lg + 1) * 8]
            e2 = p.add("dve", lambda v, hsc=hsc, hdst=hdst, m=mt:
                       v.tensor_tensor(
                           hdst.rearrange("p (l h) -> p l h", l=8),
                           hsc.rearrange("p (l h) -> p l h", l=8),
                           m.rearrange("p (l o) -> p l o", o=1).broadcast_to(
                               [128, 8, 64]),
                           ALU.mult), deps=[e1])
            prev_hsc[lg] = e2
            hfm = ctx.SMB[:, 2048 + lg * 512:2048 + (lg + 1) * 512]

            def ftr(te, hdst=hdst, lg=lg):
                last = None
                for j in range(4):
                    last = te.transpose(
                        ctx.PSB[6 + lg][:, j * 128:(j + 1) * 128],
                        hdst[:, j * 128:(j + 1) * 128], ctx.c["ident_bf"])
                return last
            tr = ctx.mm(ftr, [6 + lg], deps=[e2])
            cp = ctx.ev("dve", lambda v, hfm=hfm, lg=lg:
                        v.tensor_copy(hfm, ctx.PSB[6 + lg][:, 0:512]),
                        [6 + lg], deps=[tr])
            hb.append((hfm, cp))

        def fo(te, t=t, hb=hb):
            last = None
            for n in range(2):
                first = True
                for lg in range(2):
                    for c in range(4):
                        last = te.matmul(
                            ctx.PS[4 + n][:, 0:512],
                            hb[lg][0][:, c * 128:(c + 1) * 128],
                            ctx.FMC[:, 2 * T + lg * 4096 + c * 1024 + n * 512:
                                    2 * T + lg * 4096 + c * 1024 + n * 512 + 512],
                            start=first, stop=False)
                        first = False
                last = te.matmul(ctx.PS[4 + n][:, 0:512],
                                 msk_fm[:, t * 128:(t + 1) * 128],
                                 ctx.B2B[mb:mb + 16, n * 512:(n + 1) * 512],
                                 start=False, stop=True)
            return last
        mmo = ctx.mm(fo, [4, 5], deps=[h[1] for h in hb])
        if out_mode == "tm_hbm":
            vt = ctx.SMB[:, 4096 + par * 1024:4096 + par * 1024 + 1024]
            ev = [ctx.ev("dve", lambda v, n=n, vt=vt:
                         v.tensor_copy(vt[:, n * 512:(n + 1) * 512],
                                       ctx.PS[4 + n][:, 0:512]),
                         [4 + n], deps=[mmo, prev_st[par]]) for n in range(2)]
            st = p.dma("sp", out[t * 128:(t + 1) * 128, :], vt, deps=ev)
            prev_st[par] = st
        else:
            for n in range(2):
                sc = ctx.SMB[:, 4096 + n * 512:4096 + (n + 1) * 512]
                e1 = ctx.ev("dve", lambda v, sc=sc, n=n:
                            v.tensor_copy(sc, ctx.PS[4 + n][:, 0:512]),
                            [4 + n], deps=[mmo])

                def wr(v, j, psb, t=t, n=n):
                    return v.tensor_copy(out[:, fmcols(n * 4 + j, t * 128, 128)],
                                         psb)
                tm_to_fm(ctx, sc, wr, t, 4, deps=[e1], banks=(6 + n,) * 2)
    p.barrier()


def flash_l0(ctx, W, xa, x_out, qh, vh):
    p = ctx.p
    p.barrier()
    scale = 1.0 / 32.0
    prev_q = None
    prev_v = [None, None]
    prev_pv = [None, None]
    prev_st = None
    for t in range(NT):
        qtm = ctx.SMB[:, 1024:2048]
        d_q = p.dma("sp", qtm, qh[t * 128:(t + 1) * 128, :], deps=[prev_q])
        qfm = ctx.SMB[:, 0:1024]

        def wr(v, c, psb):
            return v.tensor_copy(qfm[:, c * 128:(c + 1) * 128], psb)
        tq = tm_to_fm(ctx, qtm, wr, t, 8, deps=[d_q])
        prev_q = tq
        mx = ctx.STAT[:, 0:8]
        sstage = ctx.SMB[:, 2048:6144]
        for g in range(8):
            bank = g % 2

            def fs(te, t=t, g=g, bank=bank):
                last = None
                for c in range(8):
                    last = te.matmul(ctx.PS[bank][:, 0:512],
                                     qfm[:, c * 128:(c + 1) * 128],
                                     ctx.FMB[:, fmcols(c, g * 512, 512)],
                                     start=(c == 0), stop=(c == 7))
                return last
            mmv = ctx.mm(fs, [bank], deps=[tq])
            c1 = ctx.ev("act", lambda s, g=g, bank=bank:
                        s.activation(sstage[:, g * 512:(g + 1) * 512],
                                     ctx.PS[bank][:, 0:512], AF.Identity),
                        [bank], deps=[mmv])
            ctx.ev("dve", lambda v, g=g, bank=bank:
                   v.tensor_reduce(mx[:, g:g + 1], ctx.PS[bank][:, 0:512],
                                   AX.X, ALU.max),
                   [bank], deps=[mmv, c1])
        bar1 = p.barrier()
        m_all = ctx.STAT[:, 8:9]
        e1 = p.add("dve", lambda v: v.tensor_reduce(m_all, mx, AX.X, ALU.max))
        nb = ctx.STAT[:, 9:10]
        e2 = p.add("dve", lambda v: v.tensor_scalar_mul(nb, m_all, -scale),
                   deps=[e1])
        probs = ctx.FMC[:, 0:4096]
        ssum = ctx.STAT[:, 10:11]
        e3 = p.add("act", lambda s: s.activation(probs, sstage, AF.Exp,
                                                 bias=nb, scale=scale),
                   deps=[e2])
        e3b = p.add("dve", lambda v: v.tensor_reduce(ssum, probs, AX.X,
                                                     ALU.add), deps=[e3])
        rz = ctx.STAT[:, 11:12]
        e4 = p.add("dve", lambda v: v.reciprocal(rz, ssum), deps=[e3b])
        lastpv = None
        for tk in range(NT):
            vpar = tk % 2
            vt = ctx.WS[:, vpar * 1024:(vpar + 1) * 1024]
            d_v = p.dma("sp", vt, vh[tk * 128:(tk + 1) * 128, :],
                        deps=[prev_v[vpar], prev_pv[vpar]], nobarrier=True)
            prev_v[vpar] = d_v
            bank = 6 + (tk % 2)
            trp = ctx.mm(lambda te, tk=tk, bank=bank: te.transpose(
                ctx.PSB[bank][:, 0:128],
                probs[:, tk * 128:(tk + 1) * 128], ctx.c["ident_bf"]),
                [bank], deps=[e3])
            pT = ctx.SCR.bitcast(BF16)[:, 3584 + (tk % 2) * 128:
                                       3584 + (tk % 2) * 128 + 128]
            cpp = ctx.ev("dve", lambda v, pT=pT, bank=bank:
                         v.tensor_copy(pT, ctx.PSB[bank][:, 0:128]),
                         [bank], deps=[trp])

            def fpv(te, pT=pT, vt=vt, tk=tk):
                last = None
                for n in range(2):
                    last = te.matmul(ctx.PS[4 + n][:, 0:512], pT,
                                     vt[:, n * 512:(n + 1) * 512],
                                     start=(tk == 0), stop=(tk == NT - 1))
                return last
            lastpv = ctx.mm(fpv, [4, 5], deps=[cpp, d_v])
            prev_pv[vpar] = lastpv
        xt = ctx.SCR[:, 0:1024]
        d_x = p.dma("sp", xt, xa[t * 128:(t + 1) * 128, :], deps=[prev_st])
        ev = [ctx.ev("dve", lambda v, n=n:
                     v.scalar_tensor_tensor(xt[:, n * 512:(n + 1) * 512],
                                            ctx.PS[4 + n][:, 0:512], rz,
                                            xt[:, n * 512:(n + 1) * 512],
                                            ALU.mult, ALU.add),
                     [4 + n], deps=[e4, d_x, lastpv]) for n in range(2)]
        st = p.dma("sp", x_out[t * 128:(t + 1) * 128, :], xt, deps=ev)
        prev_st = st
        p.barrier()


B0_STAGE = "full"
DBG_STAT = False


def dump_fm(ctx, fm_src_fn, nch, out_hbm):
    p = ctx.p
    for t in range(NT):
        outs = ctx.SCR[:, 0:nch * 128]

        def wr(v, c, psb, outs=outs):
            return v.tensor_copy(outs[:, c * 128:(c + 1) * 128], psb)
        lastcp = None
        for c in range(nch):
            bank = 6 + c % 2
            tr = ctx.mm(lambda te, c=c, t=t, bank=bank: te.transpose(
                ctx.PSB[bank][:, 0:128], fm_src_fn(c, t * 128, 128),
                ctx.c["ident_bf"]), [bank], deps=[lastcp])
            lastcp = ctx.ev("dve", lambda v, c=c, bank=bank:
                            v.tensor_copy(outs[:, c * 128:(c + 1) * 128],
                                          ctx.PSB[bank][:, 0:128]),
                            [bank], deps=[tr])
        p.dma("sp", out_hbm[t * 128:(t + 1) * 128, 0:nch * 128], outs,
              deps=[lastcp])
        p.barrier()


def emit_block0(ctx, W, x_in, x_out, mk_scratch):
    p = ctx.p
    emit_ln(ctx, x_in,
            stat_dump=(globals().get("_dbg_ap") if DBG_STAT else
                       (x_out if B0_STAGE == "lnstat" else None)),
            tm_dump=(x_out if B0_STAGE == "lntm" else None))
    if B0_STAGE in ("lnstat", "lntm"):
        return
    if B0_STAGE == "ln":
        dump_fm(ctx, lambda c, t0, n: ctx.XNFM[:, fmcols(c, t0, n)], 8, x_out)
        return
    xa = mk_scratch("b0_xa", [T, D], F32) if B0_STAGE != "attn" else x_out
    l0_attn(ctx, W, x_in, xa)
    if B0_STAGE == "attn":
        return
    fff_routing(ctx, W, "q", ctx.MSKQ, 0)
    fff_routing(ctx, W, "k", ctx.MSKK, 512)
    fff_routing(ctx, W, "v", ctx.MSKV, 1024)
    qh = mk_scratch("b0_q", [T, D], BF16)
    vh = mk_scratch("b0_v", [T, D], BF16)
    fff_apply(ctx, W, "v", ctx.MSKV, 1024, "tm_hbm", vh)
    if B0_STAGE == "fffv":
        for t in range(NT):
            sc = ctx.SCR[:, 0:1024]
            d1 = p.dma("sp", ctx.SMB[:, 0:1024], vh[t * 128:(t + 1) * 128, :])
            d2 = p.add("dve", lambda v, sc=sc: v.tensor_copy(sc, ctx.SMB[:, 0:1024]),
                       deps=[d1])
            p.dma("sp", x_out[t * 128:(t + 1) * 128, :], sc, deps=[d2])
            p.barrier()
        return
    fff_apply(ctx, W, "q", ctx.MSKQ, 0, "tm_hbm", qh)
    fff_apply(ctx, W, "k", ctx.MSKK, 512, "fm", ctx.FMB)
    if B0_STAGE == "fffk":
        dump_fm(ctx, lambda c, t0, n: ctx.FMB[:, fmcols(c, t0, n)], 8, x_out)
        return
    flash_l0(ctx, W, xa, x_out, qh, vh)


def l1_attn(ctx, W, x_in, xa, mk):
    p = ctx.p
    p.barrier()
    Wm = W["L1_attn_W"]
    bm = W["L1_attn_b"]
    q_hbm = mk("b1_q", [128, 8 * T], BF16)
    k_hbm = mk("b1_k", [128, 8 * T], BF16)
    vx_hbm = mk("b1_vx", [T, 1040], BF16)
    o_hbm = mk("b1_o", [T, 1040], BF16)
    xnfm_hbm = mk("b1_xnfm", [128, 8 * T], BF16)

    bvr = ctx.FMC[0:1, 8192:8192 + 1024]
    for part, soff, zoff in (("q", 24, 0), ("k", 32, 64)):
        for e in range(8):
            col = ctx.STAT[:, soff + e:soff + e + 1]
            p.dma("sp", col,
                  bm.rearrange("(h z) -> h z", z=192)[2 * e:2 * e + 2,
                                                      zoff:zoff + 64])
    p.dma("pool", bvr.rearrange("o (h z) -> o h z", h=16),
          bm.rearrange("(h z) -> h z", z=192)[:, 128:192].unsqueeze(0))

    def stage_wpart(zoff):
        for c in range(8):
            p.dma("pool", ctx.FMC[:, c * 1024:(c + 1) * 1024],
                  Wm.rearrange("(c p) (h z) -> p c h z", p=128, z=192)[
                      :, c, :, zoff:zoff + 64])

    def wacc(c, e0, ew):
        return ctx.FMC[:, c * 1024 + e0:c * 1024 + e0 + ew]

    def proj_dump(dst_hbm, bias_soff):
        for g in range(8):
            evs = []
            for e in range(8):
                bank = 4 + e % 2

                def f(te, e=e, g=g, bank=bank):
                    last = None
                    for c in range(8):
                        last = te.matmul(ctx.PS[bank][:, 0:512],
                                         wacc(c, e * 128, 128),
                                         ctx.XNFM[:, fmcols(c, g * 512, 512)],
                                         start=(c == 0), stop=(c == 7))
                    return last
                mmv = ctx.mm(f, [bank])
                evs.append(ctx.ev("dve", lambda v, e=e, bank=bank:
                                  v.tensor_scalar_add(
                                      ctx.SMB[:, e * 512:(e + 1) * 512],
                                      ctx.PS[bank][:, 0:512],
                                      ctx.STAT[:, bias_soff + e:bias_soff + e + 1]),
                                  [bank], deps=[mmv]))
            p.dma("sp",
                  dst_hbm.rearrange("p (c t) -> p c t", c=8)[:, :, g * 512:(g + 1) * 512],
                  ctx.SMB[:, 0:4096].rearrange("p (c t) -> p c t", c=8), deps=evs)
            p.barrier()

    p.barrier()
    stage_wpart(0)
    p.barrier()
    proj_dump(q_hbm, 24)
    stage_wpart(128)
    p.add("dve", lambda v: v.memset(
        ctx.SMB[:, 0:2080].rearrange("p (s h z) -> p s h z", s=2, z=65)[
            :, :, :, 64:65], 1.0))
    p.barrier()
    prev_v = {0: None, 1: None}
    for t in range(NT):
        par = t % 2

        def fv(te, t=t):
            last = None
            for n in range(2):
                for c in range(8):
                    last = te.matmul(ctx.PS[4 + n][:, 0:512],
                                     ctx.XNFM[:, fmcols(c, t * 128, 128)],
                                     wacc(c, n * 512, 512),
                                     start=(c == 0), stop=False)
                last = te.matmul(ctx.PS[4 + n][:, 0:512], ctx.c["ones1_bf"],
                                 bvr[:, n * 512:(n + 1) * 512],
                                 start=False, stop=True)
            return last
        mmv = ctx.mm(fv, [4, 5])
        vt = ctx.SMB[:, par * 1040:par * 1040 + 1040]
        ev = []
        for n in range(2):
            ev.append(ctx.ev("dve", lambda v, n=n, vt=vt:
                             v.tensor_copy(
                                 vt.rearrange("p (h z) -> p h z", z=65)[
                                     :, n * 8:(n + 1) * 8, 0:64],
                                 ctx.PS[4 + n][:, 0:512].rearrange(
                                     "p (h z) -> p h z", z=64)),
                             [4 + n], deps=[mmv, prev_v[par]]))
        prev_v[par] = p.dma("sp", vx_hbm[t * 128:(t + 1) * 128, :], vt, deps=ev)
    p.barrier()
    stage_wpart(64)
    p.barrier()
    proj_dump(k_hbm, 32)
    for i in range(4):
        p.dma("sp", xnfm_hbm[:, i * 8192:(i + 1) * 8192],
              ctx.XNFM[:, i * 8192:(i + 1) * 8192])
    p.barrier()

    kext = ctx.SMB[:, 2048:6144]
    qstage = ctx.WS[:, 1536:2048]
    zq = p.add("dve", lambda v: v.memset(qstage[64:128, :], 0.0))
    for half in range(2):
        p.barrier()
        for i in range(2):
            p.dma("sp", ctx.XNFM[:, i * 8192:(i + 1) * 8192],
                  k_hbm[:, half * 16384 + i * 8192:half * 16384 + (i + 1) * 8192])
        for i in range(4):
            p.dma("sp",
                  ctx.FMB[:, 0:16640].rearrange("p (t z) -> p t z", z=520)[:, i * 8:(i + 1) * 8],
                  vx_hbm.rearrange("(t p) (w z) -> p w t z", p=128, w=2)[
                      :, half, i * 8:(i + 1) * 8])
        z1 = p.add("dve", lambda v: v.memset(kext[64:128, :], 0.0))
        z2 = p.add("dve", lambda v: v.memset(kext[64:65, :], 1.0), deps=[z1])
        p.barrier()
        for g in range(8):
            d_q = p.dma("sp", ctx.SMB[:, 0:2048].rearrange("p (c t) -> p c t", c=4),
                        q_hbm.rearrange("p (c t) -> p c t", c=8)[
                            :, half * 4:(half + 1) * 4, g * 512:(g + 1) * 512])
            p.barrier()
            prev_head_pv = None
            for h in range(8):
                hp_l, hr = h // 2, (h % 2) * 64
                kb = p.add("dve", lambda v, hp_l=hp_l, hr=hr:
                           v.tensor_copy(kext[0:64, :],
                                         ctx.XNFM[hr:hr + 64,
                                                  hp_l * 4096:(hp_l + 1) * 4096]),
                           deps=[prev_head_pv])
                mcols = []
                for tl in range(4):
                    mx = ctx.STAT[:, 48 + tl * 9:48 + tl * 9 + 8]
                    red = []
                    for tkg in range(8):
                        bank = tkg % 2

                        def fs(te, tl=tl, tkg=tkg, bank=bank, hp_l=hp_l, hr=hr):
                            return te.matmul(
                                ctx.PS[bank][:, 0:512],
                                ctx.SMB[hr:hr + 64,
                                        hp_l * 512 + tl * 128:hp_l * 512 + tl * 128 + 128],
                                ctx.XNFM[hr:hr + 64,
                                         hp_l * 4096 + tkg * 512:
                                         hp_l * 4096 + (tkg + 1) * 512],
                                start=True, stop=True)
                        mmv = ctx.mm(fs, [bank])
                        red.append(ctx.ev("dve", lambda v, mx=mx, tkg=tkg, bank=bank:
                                          v.tensor_reduce(mx[:, tkg:tkg + 1],
                                                          ctx.PS[bank][:, 0:512],
                                                          AX.X, ALU.max),
                                          [bank], deps=[mmv]))
                    mc = ctx.STAT[:, 48 + tl * 9 + 8:48 + tl * 9 + 9]
                    mcols.append(p.add("dve", lambda v, mc=mc, mx=mx:
                                       v.tensor_reduce(mc, mx, AX.X, ALU.max),
                                       deps=red))
                qb = p.add("dve", lambda v, hp_l=hp_l, hr=hr:
                           v.tensor_copy(qstage[0:64, :],
                                         ctx.SMB[hr:hr + 64,
                                                 hp_l * 512:(hp_l + 1) * 512]),
                           deps=mcols + [zq, prev_head_pv])
                mrow = []
                for tl in range(4):
                    mc = ctx.STAT[:, 48 + tl * 9 + 8:48 + tl * 9 + 9]
                    tr = ctx.mm(lambda te, mc=mc: te.transpose(
                        ctx.PS[5][0:1, 0:128], mc, ctx.c["identf_sb"]),
                        [5], deps=[mcols[tl], qb])
                    mrow.append(ctx.ev("dve", lambda v, tl=tl:
                                       v.tensor_scalar_mul(
                                           qstage[64:65, tl * 128:(tl + 1) * 128],
                                           ctx.PS[5][0:1, 0:128], -1.0),
                                       [5], deps=[tr]))
                lastpv = None
                prev_pt = [None, None]
                for tkt in range(NT):
                    bank = 2 + tkt % 2

                    def fst(te, tkt=tkt, bank=bank):
                        return te.matmul(ctx.PS[bank][:, 0:512],
                                         kext[:, tkt * 128:(tkt + 1) * 128],
                                         qstage, start=True, stop=True)
                    mmv = ctx.mm(fst, [bank], deps=[kb] + mrow)
                    pt = ctx.WS[:, (tkt % 2) * 512:(tkt % 2) * 512 + 512]
                    ex = ctx.ev("act", lambda s, pt=pt, bank=bank:
                                s.activation(pt, ctx.PS[bank][:, 0:512], AF.Exp,
                                             scale=8.0),
                                [bank], deps=[mmv, prev_pt[tkt % 2]])

                    def fpv(te, pt=pt, tkt=tkt, h=h):
                        return te.matmul(ctx.PS[4][0:65, 0:512],
                                         ctx.FMB[:, tkt * 520 + h * 65:
                                                 tkt * 520 + h * 65 + 65],
                                         pt, start=(tkt == 0), stop=(tkt == NT - 1))
                    lastpv = ctx.mm(fpv, [4], deps=[ex])
                    prev_pt[tkt % 2] = lastpv
                prev_head_pv = lastpv
                oevac = ctx.WS[0:65, 1024:1536]
                e1 = ctx.ev("dve", lambda v, oevac=oevac:
                            v.tensor_copy(oevac, ctx.PS[4][0:65, 0:512]),
                            [4], deps=[lastpv])
                for tl in range(4):
                    bank = 6 + tl % 2
                    tr = ctx.mm(lambda te, tl=tl, bank=bank, oevac=oevac:
                                te.transpose(ctx.PSB[bank][0:128, 0:65],
                                             oevac[0:65, tl * 128:(tl + 1) * 128],
                                             ctx.c["ident_bf"][0:65, 0:65]),
                                [bank], deps=[e1])
                    ctx.ev("dve", lambda v, tl=tl, h=h, bank=bank:
                           v.tensor_copy(ctx.FMC[:, tl * 520 + h * 65:
                                                 tl * 520 + (h + 1) * 65],
                                         ctx.PSB[bank][0:128, 0:65]),
                           [bank], deps=[tr])
            p.barrier()
            p.dma("sp",
                  o_hbm.rearrange("(g s p) z -> g p s z", g=8, p=128)[g]
                  [:, :, half * 520:(half + 1) * 520],
                  ctx.FMC[:, 0:4 * 520].rearrange("p (s w) -> p s w", s=4))
            p.barrier()
    prevst = {0: None, 1: None}
    for t in range(NT):
        par = t % 2
        ot = ctx.SMB[:, par * 1040:par * 1040 + 1040]
        d_o = p.dma("sp", ot, o_hbm[t * 128:(t + 1) * 128, :],
                    deps=[prevst[par]])
        rz = ctx.STAT[:, 24 + par * 16:24 + par * 16 + 16]
        e1 = p.add("dve", lambda v, ot=ot, rz=rz:
                   v.reciprocal(rz, ot.rearrange("p (h z) -> p h z", z=65)[
                       :, :, 64:65]),
                   deps=[d_o])
        xt = ctx.SCR[:, par * 1024:par * 1024 + 1024]
        d_x = p.dma("sp", xt, x_in[t * 128:(t + 1) * 128, :],
                    deps=[prevst[par]])
        tn = ctx.SMB[:, 2080 + par * 1024:2080 + par * 1024 + 1024]
        e2 = p.add("dve", lambda v, ot=ot, rz=rz, tn=tn:
                   v.tensor_tensor(
                       tn.rearrange("p (h z) -> p h z", z=64),
                       ot.rearrange("p (h z) -> p h z", z=65)[:, :, 0:64],
                       rz.rearrange("p (h o) -> p h o", o=1).broadcast_to(
                           [128, 16, 64]),
                       ALU.mult),
                   deps=[e1, d_o])
        e3 = p.add("dve", lambda v, xt=xt, tn=tn:
                   v.tensor_add(xt, xt, tn), deps=[e2, d_x])
        prevst[par] = p.dma("sp", xa[t * 128:(t + 1) * 128, :], xt, deps=[e3])
    p.barrier()


def l1_ffn(ctx, W, x_in, x_out, mk):
    p = ctx.p
    p.barrier()
    xnfm_hbm = mk("b1_xnfm", [128, 8 * T], BF16)
    for i in range(4):
        p.dma("sp", ctx.XNFM[:, i * 8192:(i + 1) * 8192],
              xnfm_hbm[:, i * 8192:(i + 1) * 8192])
    t1, w1 = load_w_nat(ctx, W["L1_ffn_W1"], ctx.SMB, 0)
    t2, w2 = load_w_nat(ctx, W["L1_ffn_W2"], ctx.WS, 0)
    b1c = ctx.STAT[:, 24:28]
    for e in range(4):
        p.dma("sp", b1c[:, e:e + 1],
              W["L1_ffn_b1"][e * 128:(e + 1) * 128].unsqueeze(1))
    b2r = ctx.WS[0:1, 128:136]
    p.dma("pool", b2r, W["L1_ffn_b2"].unsqueeze(0))
    p.barrier()
    for e in range(4):
        for g in range(8):
            bank = 4 + (e * 8 + g) % 2

            def f(te, e=e, g=g, bank=bank):
                last = None
                for c in range(8):
                    last = te.matmul(ctx.PS[bank][:, 0:512], w1(c, e * 128, 128),
                                     ctx.XNFM[:, fmcols(c, g * 512, 512)],
                                     start=(c == 0), stop=(c == 7))
                return last
            mmv = ctx.mm(f, [bank])
            ctx.ev("act", lambda s, e=e, g=g, bank=bank:
                   s.activation(ctx.FMC[:, e * 4096 + g * 512:e * 4096 + (g + 1) * 512],
                                ctx.PS[bank][:, 0:512], AF.Relu,
                                bias=b1c[:, e:e + 1]),
                   [bank], deps=[mmv])
    p.barrier()
    for t in range(NT):
        bank = t % 2

        def f(te, t=t, bank=bank):
            last = None
            for c in range(4):
                last = te.matmul(ctx.PS[bank][:, 0:8],
                                 ctx.FMC[:, c * 4096 + t * 128:c * 4096 + t * 128 + 128],
                                 w2(c, 0, 8), start=(c == 0), stop=False)
            return te.matmul(ctx.PS[bank][:, 0:8], ctx.c["ones1_bf"], b2r,
                             start=False, stop=True)
        mmv = ctx.mm(f, [bank])
        ctx.ev("dve", lambda v, t=t, bank=bank:
               v.tensor_copy(ctx.STAT[:, 128 + t * 8:128 + (t + 1) * 8],
                             ctx.PS[bank][:, 0:8]),
               [bank], deps=[mmv])
    p.barrier()
    LOG = ctx.STAT[:, 128:384]
    RMAX = ctx.STAT[:, 64:96]
    PRIO = ctx.SCR[:, 1280:1536]
    PMAX = ctx.STAT[:, 96:128]
    MASKS = ctx.SCR[:, 1024:1280]
    s1 = p.add("dve", lambda v: v.tensor_reduce(
        RMAX.rearrange("p (t o) -> p t o", o=1),
        LOG.rearrange("p (t e) -> p t e", e=8), AX.X, ALU.max))
    s2 = p.add("dve", lambda v: v.tensor_tensor(
        PRIO.rearrange("p (t e) -> p t e", e=8),
        LOG.rearrange("p (t e) -> p t e", e=8),
        RMAX.rearrange("p (t o) -> p t o", o=1).broadcast_to([128, 32, 8]),
        ALU.is_ge), deps=[s1])
    s3 = p.add("dve", lambda v: v.tensor_mul(PRIO, PRIO, ctx.c["crow8x32"]),
               deps=[s2])
    s4 = p.add("dve", lambda v: v.tensor_reduce(
        PMAX.rearrange("p (t o) -> p t o", o=1),
        PRIO.rearrange("p (t e) -> p t e", e=8), AX.X, ALU.max), deps=[s3])
    s5 = p.add("dve", lambda v: v.tensor_tensor(
        MASKS.rearrange("p (t e) -> p t e", e=8),
        PRIO.rearrange("p (t e) -> p t e", e=8),
        PMAX.rearrange("p (t o) -> p t o", o=1).broadcast_to([128, 32, 8]),
        ALU.is_equal), deps=[s4])
    p.barrier()
    WE = W["L1_ffn_expert_W"]
    BE = W["L1_ffn_expert_b"]
    p.dma("pool", ctx.FMB[0:1, 16384:16384 + 8192],
          BE.rearrange("e d -> (e d)").unsqueeze(0))
    wtok = [None, None]

    def load_we(e, par, deps, nobarrier=True):
        toks = []
        for c in range(8):
            toks.append(p.dma("pool",
                              ctx.FMB[:, par * 8192 + c * 1024:
                                      par * 8192 + (c + 1) * 1024],
                              WE[e, c * 128:(c + 1) * 128, :], deps=deps,
                              nobarrier=nobarrier))
        return toks
    wtok[0] = load_we(0, 0, [], nobarrier=False)
    for grp in range(4):
        for e in range(8):
            par = (grp * 8 + e) % 2
            nxt = grp * 8 + e + 1
            last_mm = None
            for tl in range(8):
                t = grp * 8 + tl

                def f(te, t=t, par=par, e=e):
                    last = None
                    for n in range(2):
                        for c in range(8):
                            last = te.matmul(
                                ctx.PS[4 + n][:, 0:512],
                                ctx.XNFM[:, fmcols(c, t * 128, 128)],
                                ctx.FMB[:, par * 8192 + c * 1024 + n * 512:
                                        par * 8192 + c * 1024 + n * 512 + 512],
                                start=(c == 0), stop=False)
                        last = te.matmul(
                            ctx.PS[4 + n][:, 0:512], ctx.c["ones1_bf"],
                            ctx.FMB[0:1, 16384 + e * 1024 + n * 512:
                                    16384 + e * 1024 + n * 512 + 512],
                            start=False, stop=True)
                    return last
                mmv = ctx.mm(f, [4, 5], deps=wtok[par])
                last_mm = mmv
                for n in range(2):
                    acc = ctx.FMB[:, 24576 + tl * 1024 + n * 512:
                                  24576 + tl * 1024 + n * 512 + 512]
                    mcol = MASKS[:, t * 8 + e:t * 8 + e + 1]
                    if e == 0:
                        ctx.ev("dve", lambda v, acc=acc, mcol=mcol, n=n:
                               v.tensor_scalar_mul(acc, ctx.PS[4 + n][:, 0:512],
                                                   mcol),
                               [4 + n], deps=[mmv])
                    else:
                        ctx.ev("dve", lambda v, acc=acc, mcol=mcol, n=n:
                               v.scalar_tensor_tensor(acc, ctx.PS[4 + n][:, 0:512],
                                                      mcol, acc, ALU.mult, ALU.add),
                               [4 + n], deps=[mmv])
            if nxt < 32:
                wtok[1 - par] = load_we(nxt % 8, 1 - par, [last_mm])
        prevst = None
        for tl in range(8):
            t = grp * 8 + tl
            xt = ctx.SCR[:, 0:1024]
            d_x = p.dma("sp", xt, x_in[t * 128:(t + 1) * 128, :], deps=[prevst])
            e1 = p.add("dve", lambda v, xt=xt, tl=tl:
                       v.tensor_add(xt, xt,
                                    ctx.FMB[:, 24576 + tl * 1024:
                                            24576 + (tl + 1) * 1024]),
                       deps=[d_x])
            prevst = p.dma("sp", x_out[t * 128:(t + 1) * 128, :], xt, deps=[e1])
        p.barrier()


def emit_block1(ctx, W, x_in, x_out, mk_scratch):
    emit_ln(ctx, x_in)
    xa = mk_scratch("b1_xa", [T, D], F32)
    l1_attn(ctx, W, x_in, xa, mk_scratch)
    l1_ffn(ctx, W, xa, x_out, mk_scratch)


def emit_rms(ctx, y_hbm, gamma):
    p = ctx.p
    p.barrier()
    grow = ctx.SCR[0:1, 1024:2048]
    d_g = p.dma("sp", grow, gamma.unsqueeze(0))
    GB = ctx.SMB[:, 4096:5120]
    for n0 in (0, 512):
        mmv = ctx.mm(lambda te, n0=n0: te.matmul(
            ctx.PS[6][:, 0:512], ctx.c["ones1_f32"], grow[0:1, n0:n0 + 512],
            start=True, stop=True), [6], deps=[d_g])
        ctx.ev("dve", lambda v, n0=n0: v.tensor_copy(
            GB[:, n0:n0 + 512], ctx.PS[6][:, 0:512]), [6], deps=[mmv])
    p.barrier()
    prev = {0: None, 1: None}
    prev_sq = {0: None, 1: None}
    for t in range(NT):
        par = t % 2
        xt = ctx.SCR[:, par * 1024:par * 1024 + D]
        d_x = p.dma("sp", xt, y_hbm[t * 128:(t + 1) * 128, :],
                    deps=[prev[par], prev_sq[par]])
        sq = ctx.STAT[:, par * 2:par * 2 + 1]
        rst = ctx.STAT[:, par * 2 + 1:par * 2 + 2]
        sqo = ctx.SMB[:, 2048 + par * 1024:2048 + par * 1024 + D]
        a1 = p.add("act", lambda s, i=xt, o=sqo:
                   s.activation(o, i, AF.Square), deps=[d_x])
        prev_sq[par] = a1
        a1b = p.add("dve", lambda v, i=sqo, o=sq:
                    v.tensor_reduce(o, i, AX.X, ALU.add), deps=[a1])
        a2 = p.add("act", lambda s, r=rst, q=sq:
                   s.activation(r, q, AF.Ln, scale=1.0 / D), deps=[a1b])
        a3 = p.add("act", lambda s, r=rst:
                   s.activation(r, r, AF.Exp, scale=-0.5), deps=[a2])
        xnb = ctx.SMB[:, par * 1024:par * 1024 + D]
        a4 = p.add("dve", lambda v, o=xnb, i=xt, r=rst:
                   v.scalar_tensor_tensor(o, i, r, GB, ALU.mult, ALU.mult),
                   deps=[a3, d_x])

        def wr(v, c, psb, t=t):
            return v.tensor_copy(ctx.XNFM[:, fmcols(c, t * 128, 128)], psb)
        prev[par] = tm_to_fm(ctx, xnb, wr, t, 8, deps=[a4])
    return p.barrier()


def resid_evac(ctx, p, y_hbm, out_hbm, t, banks, prevst, extra=()):
    par = t % 2
    xt = ctx.SCR[:, par * 1024:par * 1024 + 1024]
    d_y = p.dma("sp", xt, y_hbm[t * 128:(t + 1) * 128, :],
                deps=[prevst[par]] + list(extra))
    ev = []
    for n in range(2):
        ev.append(ctx.ev("dve", lambda v, n=n, xt=xt, b=banks[n]:
                         v.tensor_add(xt[:, n * 512:(n + 1) * 512],
                                      xt[:, n * 512:(n + 1) * 512],
                                      ctx.PS[b][:, 0:512]),
                         [banks[n]], deps=[d_y]))
    st = p.dma("sp", out_hbm[t * 128:(t + 1) * 128, :], xt, deps=ev)
    prevst[par] = st
    ctx._last_resid = (ev, xt)
    return st


def l2_hawk(ctx, W, y_hbm, y2_hbm, mk):
    p = ctx.p
    gate_hbm = mk("b2_gate", [128, 12 * T], BF16)
    hc_hbm = mk("b2_hc", [128, 12 * T], BF16)
    al_hbm = mk("b2_al", [128, 12 * T], F32)
    xs_hbm = mk("b2_xs", [128, 12 * T], BF16)
    gh_hbm = mk("b2_gh", [128, 12 * T], BF16)
    emit_rms(ctx, y_hbm, W["L2_attn_gn_hawk"])
    tw, winT = load_wT(ctx, W["L2_attn_hawk_Win"], ctx.FMB, 0)
    acol = ctx.STAT[:, 64:76]
    t1c = ctx.STAT[:, 76:88]
    for c in range(12):
        p.dma("sp", acol[:, c:c + 1],
              W["L2_attn_hawk_fb"][c * 128:(c + 1) * 128].unsqueeze(1))
        p.dma("sp", ctx.STAT[:, 88 + c * 4:88 + (c + 1) * 4],
              W["L2_attn_hawk_cw"][c * 128:(c + 1) * 128, 0, :])
        p.dma("sp", ctx.STAT[:, 136 + c:137 + c],
              W["L2_attn_hawk_cb"][c * 128:(c + 1) * 128].unsqueeze(1))
        p.dma("sp", ctx.STAT[:, 148 + c:149 + c],
              W["L2_attn_hawk_bg"][c * 128:(c + 1) * 128].unsqueeze(1))
        p.dma("sp", ctx.STAT[:, 160 + c:161 + c],
              W["L2_attn_hawk_bg"][1536 + c * 128:1536 + (c + 1) * 128].unsqueeze(1))
    p.barrier()
    s1 = p.add("act", lambda s: s.activation(t1c, acol, AF.Abs))
    s2 = p.add("act", lambda s: s.activation(t1c, t1c, AF.Exp, scale=-1.0),
               deps=[s1])
    s3 = p.add("act", lambda s: s.activation(t1c, t1c, AF.Ln, bias=1.0),
               deps=[s2])
    s4 = p.add("dve", lambda v: v.tensor_scalar_max(acol, acol, 0.0), deps=[s3])
    s5 = p.add("dve", lambda v: v.tensor_add(acol, acol, t1c), deps=[s4])
    s6 = p.add("dve", lambda v: v.tensor_scalar_mul(acol, acol, -8.0), deps=[s5])
    p.barrier()

    prevg = {0: None, 1: None}
    for c in range(12):
        for g in range(8):
            par = (c * 8 + g) % 2
            bank = 4 + par

            def f(te, c=c, g=g, bank=bank):
                last = None
                for d in range(8):
                    last = te.matmul(ctx.PS[bank][:, 0:512],
                                     winT(d, c * 128, 128),
                                     ctx.XNFM[:, fmcols(d, g * 512, 512)],
                                     start=(d == 0), stop=(d == 7))
                return last
            mmv = ctx.mm(f, [bank])
            stg = ctx.SMB[:, par * 512:par * 512 + 512]
            e1 = ctx.ev("dve", lambda v, stg=stg, bank=bank:
                        v.tensor_copy(stg, ctx.PS[bank][:, 0:512]),
                        [bank], deps=[mmv, prevg[par]])
            prevg[par] = p.dma("sp",
                               gate_hbm[:, c * T + g * 512:c * T + (g + 1) * 512],
                               stg, deps=[e1])
    p.barrier()
    zh = p.add("dve", lambda v: v.memset(ctx.SMB[:, 1024:1028], 0.0))
    for c in range(12):
        evs = []
        for g in range(8):
            bank = 4 + g % 2

            def f(te, c=c, g=g, bank=bank):
                last = None
                for d in range(8):
                    last = te.matmul(ctx.PS[bank][:, 0:512],
                                     winT(d, 1536 + c * 128, 128),
                                     ctx.XNFM[:, fmcols(d, g * 512, 512)],
                                     start=(d == 0), stop=(d == 7))
                return last
            mmv = ctx.mm(f, [bank])
            evs.append(ctx.ev("dve", lambda v, g=g, bank=bank:
                              v.tensor_copy(
                                  ctx.SMB[:, 1028 + g * 512:1028 + (g + 1) * 512],
                                  ctx.PS[bank][:, 0:512]),
                              [bank], deps=[mmv]))
        prevs = {0: None, 1: None}
        for s in range(8):
            par = s % 2
            acc = ctx.SCR[:, par * 512:par * 512 + 512]
            d1 = p.add("dve", lambda v, s=s, acc=acc, c=c:
                       v.tensor_scalar(acc,
                                       ctx.SMB[:, 1025 + s * 512:1537 + s * 512],
                                       ctx.STAT[:, 88 + c * 4:89 + c * 4],
                                       None, ALU.mult),
                       deps=evs + [prevs[par], zh])
            for k in range(1, 4):
                d1 = p.add("dve", lambda v, s=s, k=k, acc=acc, c=c:
                           v.scalar_tensor_tensor(
                               acc,
                               ctx.SMB[:, 1025 + s * 512 + k:1537 + s * 512 + k],
                               ctx.STAT[:, 88 + c * 4 + k:89 + c * 4 + k],
                               acc, ALU.mult, ALU.add),
                           deps=[d1])
            stg = ctx.SMB[:, par * 512:par * 512 + 512]
            d2 = p.add("dve", lambda v, stg=stg, acc=acc, c=c:
                       v.tensor_scalar_add(stg, acc,
                                           ctx.STAT[:, 136 + c:137 + c]),
                       deps=[d1])
            prevs[par] = p.dma("sp",
                               hc_hbm[:, c * T + s * 512:c * T + (s + 1) * 512],
                               stg, deps=[d2])
        p.barrier()
    for which in ("f", "i"):
        rows = slice(0, 1536) if which == "f" else slice(1536, 3072)
        bgoff = 148 if which == "f" else 160
        tw2, wgT = load_wT(ctx, W["L2_attn_hawk_Wg"][rows, :], ctx.FMB, 0)
        p.barrier()
        for g in range(8):
            d_hc = p.dma("sp",
                         ctx.FMC[:, 0:6144].rearrange("p (c z) -> p c z", z=512),
                         hc_hbm.rearrange("p (c t) -> p c t", c=12)[
                             :, :, g * 512:(g + 1) * 512])
            p.barrier()
            prevo = {0: None, 1: None}
            for c in range(12):
                par = c % 2
                bank = 4 + par

                def f(te, c=c, bank=bank):
                    last = None
                    for ci in range(12):
                        last = te.matmul(ctx.PS[bank][:, 0:512],
                                         wgT(ci, c * 128, 128),
                                         ctx.FMC[:, ci * 512:(ci + 1) * 512],
                                         start=(ci == 0), stop=(ci == 11))
                    return last
                mmv = ctx.mm(f, [bank])
                sg = ctx.SCR[:, par * 512:par * 512 + 512]
                e1 = ctx.ev("act", lambda s, sg=sg, bank=bank, c=c, bgoff=bgoff:
                            s.activation(sg, ctx.PS[bank][:, 0:512], AF.Sigmoid,
                                         bias=ctx.STAT[:, bgoff + c:bgoff + c + 1]),
                            [bank], deps=[mmv, prevo[par]])
                stg = ctx.SMB[:, par * 512:par * 512 + 512]
                if which == "f":
                    stgf = ctx.SCR[:, 1024 + par * 512:1536 + par * 512]
                    e2 = p.add("act", lambda s, sg=sg, stgf=stgf, c=c:
                               s.activation(stgf, sg, AF.Exp,
                                            scale=acol[:, c:c + 1]),
                               deps=[e1, prevo[par]])
                    prevo[par] = p.dma(
                        "sp", al_hbm[:, c * T + g * 512:c * T + (g + 1) * 512],
                        stgf, deps=[e2])
                else:
                    alc = ctx.FMC.bitcast(F32)[:, 3072 + par * 512:
                                               3584 + par * 512]
                    d_alc = p.dma("sp", alc,
                                  al_hbm[:, c * T + g * 512:c * T + (g + 1) * 512],
                                  deps=[prevo[par]], nobarrier=True)
                    sb2 = ctx.SCR[:, 1024 + par * 512:1536 + par * 512]
                    e2 = p.add("act", lambda s, alc=alc, sb2=sb2:
                               s.activation(sb2, alc, AF.Square),
                               deps=[d_alc, e1])
                    e3 = p.add("dve", lambda v, sb2=sb2:
                               v.tensor_scalar(sb2, sb2, -1.0, 1.000001,
                                               ALU.mult, ALU.add), deps=[e2])
                    e4 = p.add("act", lambda s, sb2=sb2:
                               s.activation(sb2, sb2, AF.Sqrt), deps=[e3])
                    e5 = p.add("dve", lambda v, sg=sg, sb2=sb2:
                               v.tensor_mul(sb2, sb2, sg), deps=[e4, e1])
                    e6 = p.add("dve", lambda v, stg=stg, sb2=sb2, c=c:
                               v.tensor_mul(stg, sb2,
                                            ctx.FMC[:, c * 512:(c + 1) * 512]),
                               deps=[e5, d_hc])
                    prevo[par] = p.dma(
                        "sp", xs_hbm[:, c * T + g * 512:c * T + (g + 1) * 512],
                        stg, deps=[e6])
            p.barrier()
    SMBF = ctx.SMB.bitcast(F32)
    for c in range(12):
        carry = ctx.STAT[:, 172 + c:173 + c]
        zc = p.add("dve", lambda v, carry=carry: v.memset(carry, 0.0))
        for hf in range(4):
            d_a = p.dma("sp", SMBF[:, 0:1024],
                        al_hbm[:, c * T + hf * 1024:c * T + (hf + 1) * 1024])
            d_x = p.dma("sp", ctx.SMB[:, 2048:3072],
                        xs_hbm[:, c * T + hf * 1024:c * T + (hf + 1) * 1024])
            d_g = p.dma("sp", ctx.SMB[:, 3072:4096],
                        gate_hbm[:, c * T + hf * 1024:c * T + (hf + 1) * 1024])
            p.barrier()
            sc1 = p.add("dve", lambda v, carry=carry:
                        v.tensor_tensor_scan(ctx.SCR[:, 0:1024],
                                             SMBF[:, 0:1024],
                                             ctx.SMB[:, 2048:3072],
                                             carry, ALU.mult, ALU.add),
                        deps=[zc, d_a, d_x])
            sc2 = p.add("dve", lambda v, carry=carry:
                        v.tensor_copy(carry, ctx.SCR[:, 1023:1024]), deps=[sc1])
            g1 = p.add("act", lambda s:
                       s.activation(ctx.SMB[:, 3072:4096], ctx.SMB[:, 3072:4096],
                                    AF.Gelu), deps=[d_g])
            g2 = p.add("dve", lambda v:
                       v.tensor_mul(ctx.SMB[:, 4096:5120], ctx.SCR[:, 0:1024],
                                    ctx.SMB[:, 3072:4096]), deps=[sc1, g1])
            p.dma("sp", gh_hbm[:, c * T + hf * 1024:c * T + (hf + 1) * 1024],
                  ctx.SMB[:, 4096:5120], deps=[g2])
            p.barrier()
    tw3, woT = load_wT(ctx, W["L2_attn_hawk_Wout"], ctx.FMC, 0)
    p.barrier()
    prevst = {0: None, 1: None}
    for st_ in range(8):
        d_gh = p.dma("sp",
                     ctx.SMB[:, 0:6144].rearrange("p (c z) -> p c z", z=512),
                     gh_hbm.rearrange("p (c t) -> p c t", c=12)[
                         :, :, st_ * 512:(st_ + 1) * 512])
        p.barrier()
        for tl in range(4):
            t = st_ * 4 + tl

            def f(te, tl=tl):
                last = None
                for n in range(2):
                    for ci in range(12):
                        last = te.matmul(ctx.PS[4 + n][:, 0:512],
                                         ctx.SMB[:, ci * 512 + tl * 128:
                                                 ci * 512 + tl * 128 + 128],
                                         woT(ci, n * 512, 512),
                                         start=(ci == 0), stop=(ci == 11))
                return last
            mmv = ctx.mm(f, [4, 5], deps=[d_gh])
            resid_evac(ctx, p, y_hbm, y2_hbm, t, (4, 5), prevst)
    p.barrier()


def l2_gmlp(ctx, W, wg_name, ws_name, gamma, y_hbm, y2_hbm):
    p = ctx.p
    emit_rms(ctx, y_hbm, gamma)
    twg, wgT = load_wT(ctx, W[wg_name], ctx.FMB, 0,
                       stage=ctx.SMB[:, 0:2048])
    tws, wsT = load_wT(ctx, W[ws_name], ctx.FMC, 0,
                       stage=ctx.SMB[:, 2048:6144])
    p.barrier()

    def slot(c):
        if c < 12:
            return ctx.SMB[:, c * 512:(c + 1) * 512]
        return ctx.WS[:, (c - 12) * 512:(c - 11) * 512]
    prevst = {0: None, 1: None}
    for g in range(8):
        sls = []
        for c in range(16):
            bg = c % 2
            bh = 2 + c % 2

            def fg(te, c=c, g=g, bg=bg):
                last = None
                for d in range(8):
                    last = te.matmul(ctx.PS[bg][:, 0:512], wgT(d, c * 128, 128),
                                     ctx.XNFM[:, fmcols(d, g * 512, 512)],
                                     start=(d == 0), stop=(d == 7))
                return last

            def fh(te, c=c, g=g, bh=bh):
                last = None
                for d in range(8):
                    last = te.matmul(ctx.PS[bh][:, 0:512],
                                     wgT(d, 2048 + c * 128, 128),
                                     ctx.XNFM[:, fmcols(d, g * 512, 512)],
                                     start=(d == 0), stop=(d == 7))
                return last
            m1 = ctx.mm(fg, [bg])
            m2 = ctx.mm(fh, [bh])
            sg = ctx.MSKT[:, (c % 2) * 512:(c % 2) * 512 + 512]
            e1 = ctx.ev("act", lambda s, sg=sg, bg=bg:
                        s.activation(sg, ctx.PS[bg][:, 0:512], AF.Gelu),
                        [bg], deps=[m1, sls[c - 2] if c >= 2 else None])
            sls.append(ctx.ev("dve", lambda v, c=c, sg=sg, bh=bh:
                              v.tensor_mul(slot(c), sg, ctx.PS[bh][:, 0:512]),
                              [bh], deps=[e1, m2]))
        for tl in range(4):
            t = g * 4 + tl
            bo = (4 + (tl % 2) * 2, 5 + (tl % 2) * 2)

            def fo(te, tl=tl, bo=bo):
                last = None
                for n in range(2):
                    for c in range(16):
                        last = te.matmul(ctx.PS[bo[n]][:, 0:512],
                                         slot(c)[:, tl * 128:tl * 128 + 128],
                                         wsT(c, n * 512, 512),
                                         start=(c == 0), stop=(c == 15))
                return last
            mmv = ctx.mm(fo, list(bo), deps=sls)
            resid_evac(ctx, p, y_hbm, y2_hbm, t, bo, prevst)
        p.barrier()


def l2_smqa(ctx, W, y_hbm, y2_hbm):
    p = ctx.p
    emit_rms(ctx, y_hbm, W["L2_attn_gn_smqa"])
    twq, wqT = load_wT(ctx, W["L2_attn_smqa_Wq"], ctx.FMB, 0,
                       stage=ctx.SMB[:, 0:2048])
    twk, wkvT = load_wT(ctx, W["L2_attn_smqa_Wkv"], ctx.FMB, 8 * 1024,
                        stage=ctx.SMB[:, 2048:4096])
    two, woT = load_wT(ctx, W["L2_attn_smqa_Wout"], ctx.FMB, 10 * 1024,
                       stage=ctx.SMB[:, 4096:6144])
    p.dma("sp", ctx.FMC[:, 0:4096], ctx.craw["rot_cos"])
    p.dma("sp", ctx.FMC[:, 4096:8192], ctx.craw["rot_sin"])
    p.barrier()

    def rotary(src_sb, bank, cos0, n, deps):
        mmv = ctx.mm(lambda te, src_sb=src_sb, bank=bank, n=n: te.matmul(
            ctx.PS[bank][:, 0:n], ctx.c["rotm"], src_sb,
            start=True, stop=True), [bank], deps=deps)
        t1 = ctx.MSKT[:, 512:512 + n]
        e1 = p.add("dve", lambda v, src_sb=src_sb, t1=t1, cos0=cos0, n=n:
                   v.tensor_mul(t1, src_sb, ctx.FMC[:, cos0:cos0 + n]),
                   deps=deps)
        t2 = ctx.MSKT[:, 1024:1024 + n]
        e2 = ctx.ev("dve", lambda v, t2=t2, bank=bank, cos0=cos0, n=n:
                    v.tensor_mul(t2, ctx.PS[bank][:, 0:n],
                                 ctx.FMC[:, 4096 + cos0:4096 + cos0 + n]),
                    [bank], deps=[mmv])
        return p.add("dve", lambda v, src_sb=src_sb, t1=t1, t2=t2:
                     v.tensor_add(src_sb, t1, t2), deps=[e1, e2])
    for g in range(8):
        def fk(te, g=g):
            last = None
            for d in range(8):
                last = te.matmul(ctx.PS[0][:, 0:512], wkvT(d, 0, 128),
                                 ctx.XNFM[:, fmcols(d, g * 512, 512)],
                                 start=(d == 0), stop=(d == 7))
            return last
        mmv = ctx.mm(fk, [0])
        kg = ctx.MSKT[:, 0:512]
        e1 = ctx.ev("dve", lambda v, kg=kg:
                    v.tensor_copy(kg, ctx.PS[0][:, 0:512]), [0], deps=[mmv])
        e3 = rotary(kg, 1, g * 512, 512, [e1])
        p.add("dve", lambda v, g=g, kg=kg:
              v.tensor_copy(ctx.FMC[:, 8192 + g * 512:8192 + (g + 1) * 512], kg),
              deps=[e3])
        p.barrier()
    for t in range(NT):
        def fv(te, t=t):
            last = None
            for d in range(8):
                last = te.matmul(ctx.PS[2][:, 0:128],
                                 ctx.XNFM[:, fmcols(d, t * 128, 128)],
                                 wkvT(d, 128, 128), start=(d == 0), stop=(d == 7))
            return last
        mmv = ctx.mm(fv, [2])
        ctx.ev("dve", lambda v, t=t: v.tensor_copy(
            ctx.SMB[:, t * 128:(t + 1) * 128], ctx.PS[2][:, 0:128]),
            [2], deps=[mmv])
    p.barrier()
    prevst = {0: None, 1: None}
    for t in range(NT):
        kl = max(0, t - 8)
        nk = t - kl + 1
        OTM = ctx.WS[:, 0:1024]
        sbanks = [0, 1, 2]
        for h in range(8):
            def fq(te, t=t, h=h):
                last = None
                for d in range(8):
                    last = te.matmul(ctx.PS[3][:, 0:128], wqT(d, h * 128, 128),
                                     ctx.XNFM[:, fmcols(d, t * 128, 128)],
                                     start=(d == 0), stop=(d == 7))
                return last
            mmq = ctx.mm(fq, [3])
            qr = ctx.SMB[:, 4096:4224]
            e1 = ctx.ev("dve", lambda v, qr=qr:
                        v.tensor_copy(qr, ctx.PS[3][:, 0:128]), [3], deps=[mmq])
            e2 = rotary(qr, 3, t * 128, 128, [e1])
            mms = []
            for ki in range(nk):
                kt = kl + ki
                bank = sbanks[ki // 4]
                off = (ki % 4) * 128

                def fs(te, kt=kt, bank=bank, off=off, qr=qr):
                    return te.matmul(ctx.PS[bank][:, off:off + 128], qr,
                                     ctx.FMC[:, 8192 + kt * 128:
                                             8192 + (kt + 1) * 128],
                                     start=True, stop=True)
                mms.append(ctx.mm(fs, [bank], deps=[e2]))
            bank_d = sbanks[(nk - 1) // 4]
            off_d = ((nk - 1) % 4) * 128
            mk1 = ctx.ev("dve", lambda v, bank_d=bank_d, off_d=off_d:
                         v.tensor_add(ctx.PS[bank_d][:, off_d:off_d + 128],
                                      ctx.PS[bank_d][:, off_d:off_d + 128],
                                      ctx.c["negu"]),
                         [bank_d], deps=mms)
            mk2 = None
            if t >= 8:
                mk2 = ctx.ev("dve", lambda v:
                             v.tensor_add(ctx.PS[0][:, 0:128],
                                          ctx.PS[0][:, 0:128], ctx.c["negl"]),
                             [0], deps=mms)
            mxc = ctx.STAT[:, 8:11]
            mdeps = [mk1, mk2] + mms
            reds = []
            for bi, bank in enumerate(sbanks):
                w = min(nk * 128 - bi * 512, 512)
                if w <= 0:
                    break
                reds.append(ctx.ev("dve", lambda v, bi=bi, bank=bank, w=w:
                                   v.tensor_reduce(mxc[:, bi:bi + 1],
                                                   ctx.PS[bank][:, 0:w],
                                                   AX.X, ALU.max),
                                   [bank], deps=mdeps))
            m_all = ctx.STAT[:, 11:12]
            r1 = p.add("dve", lambda v, nb=len(reds): v.tensor_reduce(
                m_all, mxc[:, 0:nb], AX.X, ALU.max), deps=reds)
            nb_ = ctx.STAT[:, 12:13]
            r2 = p.add("dve", lambda v: v.tensor_scalar_mul(
                nb_, m_all, -1.0 / np.sqrt(128.0)), deps=[r1])
            probs = ctx.SMB[:, 4224:5376]
            exs = []
            for bi, bank in enumerate(sbanks):
                w = min(nk * 128 - bi * 512, 512)
                if w <= 0:
                    break
                exs.append(ctx.ev("act", lambda s, bi=bi, bank=bank, w=w:
                                  s.activation(probs[:, bi * 512:bi * 512 + w],
                                               ctx.PS[bank][:, 0:w], AF.Exp,
                                               scale=1.0 / np.sqrt(128.0),
                                               bias=nb_),
                                  [bank], deps=[r2]))
            ssum = ctx.STAT[:, 13:14]
            r3 = p.add("dve", lambda v, nk=nk: v.tensor_reduce(
                ssum, probs[:, 0:nk * 128], AX.X, ALU.add), deps=exs)
            rz = ctx.STAT[:, 14:15]
            r4 = p.add("dve", lambda v: v.reciprocal(rz, ssum), deps=[r3])
            lastpv = None
            for ki in range(nk):
                kt = kl + ki
                bank = 6 + ki % 2
                tr = ctx.mm(lambda te, ki=ki, bank=bank: te.transpose(
                    ctx.PSB[bank][:, 0:128], probs[:, ki * 128:(ki + 1) * 128],
                    ctx.c["ident_bf"]), [bank], deps=exs)
                pT = ctx.SMB[:, 5376:5504]
                cp = ctx.ev("dve", lambda v, pT=pT, bank=bank:
                            v.tensor_copy(pT, ctx.PSB[bank][:, 0:128]),
                            [bank], deps=[tr, lastpv])

                def fpv(te, pT=pT, kt=kt, ki=ki, nk=nk):
                    return te.matmul(ctx.PS[4][:, 0:128], pT,
                                     ctx.SMB[:, kt * 128:(kt + 1) * 128],
                                     start=(ki == 0), stop=(ki == nk - 1))
                lastpv = ctx.mm(fpv, [4], deps=[cp])
            ctx.ev("dve", lambda v, h=h, rz=rz:
                   v.tensor_scalar(OTM[:, h * 128:(h + 1) * 128],
                                   ctx.PS[4][:, 0:128], rz, None, ALU.mult),
                   [4], deps=[lastpv, r4])
        OFM = ctx.WS[:, 1024:2048]
        lasttr = None
        for c0 in (0, 4):
            bank = 6 + c0 // 4

            def ftr(te, c0=c0, bank=bank):
                last = None
                for j in range(4):
                    last = te.transpose(ctx.PSB[bank][:, j * 128:(j + 1) * 128],
                                        OTM[:, (c0 + j) * 128:(c0 + j + 1) * 128],
                                        ctx.c["ident_bf"])
                return last
            tr = ctx.mm(ftr, [bank], deps=[lasttr])
            lasttr = ctx.ev("dve", lambda v, c0=c0, bank=bank:
                            v.tensor_copy(OFM[:, c0 * 128:(c0 + 4) * 128],
                                          ctx.PSB[bank][:, 0:512]),
                            [bank], deps=[tr])

        def fo(te):
            last = None
            for n in range(2):
                for c in range(8):
                    last = te.matmul(ctx.PS[4 + n][:, 0:512],
                                     OFM[:, c * 128:(c + 1) * 128],
                                     woT(c, n * 512, 512),
                                     start=(c == 0), stop=(c == 7))
            return last
        mmv = ctx.mm(fo, [4, 5], deps=[lasttr])
        resid_evac(ctx, p, y_hbm, y2_hbm, t, (4, 5), prevst)
    p.barrier()


def l2_ffn(ctx, W, xn_hbm, x_in, y_hbm, x_out, mk):
    p = ctx.p
    p.barrier()
    for i in range(4):
        p.dma("sp", ctx.XNFM[:, i * 8192:(i + 1) * 8192],
              xn_hbm[:, i * 8192:(i + 1) * 8192])
    t1, wrT = load_wT(ctx, W["L2_ffn_W_r"], ctx.FMB, 0,
                      stage=ctx.SMB[:, 0:2048])
    t2, wkT = load_wT(ctx, W["L2_ffn_W_k"], ctx.FMB, 8 * 1024,
                      stage=ctx.SMB[:, 2048:4096])
    t3, wvT = load_wT(ctx, W["L2_ffn_W_v"], ctx.FMB, 16 * 1024,
                      stage=ctx.SMB[:, 4096:6144])
    mur = ctx.STAT[:, 24:32]
    muk = ctx.STAT[:, 32:40]
    for c in range(8):
        p.dma("sp", mur[:, c:c + 1],
              W["L2_ffn_mu_r"][c * 128:(c + 1) * 128].unsqueeze(1))
        p.dma("sp", muk[:, c:c + 1],
              W["L2_ffn_mu_k"][c * 128:(c + 1) * 128].unsqueeze(1))
    p.barrier()
    prevst = {0: None, 1: None}
    for g in range(8):
        for c in range(8):
            if g == 0:
                stg = ctx.SMB[:, 4096:4609]
                z0 = p.add("dve", lambda v, stg=stg: v.memset(stg[:, 0:1], 0.0))
                cpy = p.add("dve", lambda v, c=c, stg=stg:
                            v.tensor_copy(stg[:, 1:513],
                                          ctx.XNFM[:, fmcols(c, 0, 512)]),
                            deps=[z0])
                cur, prv = stg[:, 1:513], stg[:, 0:512]
            else:
                cur = ctx.XNFM[:, fmcols(c, g * 512, 512)]
                prv = ctx.XNFM[:, fmcols(c, g * 512 - 1, 512)]
                cpy = None
            df = ctx.SCR[:, 0:512]
            s1 = p.add("dve", lambda v, cur=cur, prv=prv, df=df:
                       v.tensor_sub(df, cur, prv), deps=[cpy])
            s2 = p.add("dve", lambda v, c=c, df=df, prv=prv:
                       v.scalar_tensor_tensor(
                           ctx.FMC[:, c * 512:(c + 1) * 512],
                           df, muk[:, c:c + 1], prv, ALU.mult, ALU.add),
                       deps=[s1])
            s3 = p.add("dve", lambda v, c=c, df=df, prv=prv:
                       v.scalar_tensor_tensor(
                           ctx.FMC[:, 4096 + c * 512:4096 + (c + 1) * 512],
                           df, mur[:, c:c + 1], prv, ALU.mult, ALU.add),
                       deps=[s1])
        p.barrier()
        for c in range(8):
            bank = c % 2

            def fk2(te, c=c, bank=bank):
                last = None
                for d in range(8):
                    last = te.matmul(ctx.PS[bank][:, 0:512], wkT(d, c * 128, 128),
                                     ctx.FMC[:, d * 512:(d + 1) * 512],
                                     start=(d == 0), stop=(d == 7))
                return last
            mmv = ctx.mm(fk2, [bank])
            sg = ctx.SCR[:, (c % 2) * 512:(c % 2) * 512 + 512]
            e1 = ctx.ev("act", lambda s, sg=sg, bank=bank:
                        s.activation(sg, ctx.PS[bank][:, 0:512], AF.Relu),
                        [bank], deps=[mmv])
            p.add("dve", lambda v, c=c, sg=sg:
                  v.tensor_mul(ctx.SMB[:, c * 512:(c + 1) * 512], sg, sg),
                  deps=[e1])
        p.barrier()
        for tl in range(4):
            t = g * 4 + tl

            def fv2(te, tl=tl):
                last = None
                for n in range(2):
                    for c in range(8):
                        last = te.matmul(ctx.PS[4 + n][:, 0:512],
                                         ctx.SMB[:, c * 512 + tl * 128:
                                                 c * 512 + tl * 128 + 128],
                                         wvT(c, n * 512, 512),
                                         start=(c == 0), stop=(c == 7))
                return last
            mv = ctx.mm(fv2, [4, 5])

            def fr(te, tl=tl):
                last = None
                for n in range(2):
                    for c in range(8):
                        last = te.matmul(ctx.PS[2 + n][:, 0:512],
                                         ctx.FMC[:, 4096 + c * 512 + tl * 128:
                                                 4096 + c * 512 + tl * 128 + 128],
                                         wrT(c, n * 512, 512),
                                         start=(c == 0), stop=(c == 7))
                return last
            mr = ctx.mm(fr, [2, 3])
            par = t % 2
            rr = ctx.SCR[:, 0:1024]
            prev_all = [prevst[0], prevst[1]]
            e1 = [ctx.ev("act", lambda s, n=n, rr=rr:
                         s.activation(rr[:, n * 512:(n + 1) * 512],
                                      ctx.PS[2 + n][:, 0:512], AF.Sigmoid),
                         [2 + n], deps=[mr] + prev_all) for n in range(2)]
            e2 = [ctx.ev("dve", lambda v, n=n, rr=rr:
                         v.tensor_mul(rr[:, n * 512:(n + 1) * 512],
                                      rr[:, n * 512:(n + 1) * 512],
                                      ctx.PS[4 + n][:, 0:512]),
                         [4 + n], deps=e1 + [mv]) for n in range(2)]
            xt = ctx.SCR[:, 1024:2048]
            d_x = p.dma("sp", xt, x_in[t * 128:(t + 1) * 128, :],
                        deps=prev_all)
            e3 = p.add("dve", lambda v, xt=xt, rr=rr:
                       v.tensor_add(xt, xt, rr), deps=e2 + [d_x])
            yt = ctx.SMB[:, 4096:6144].bitcast(F32)[:, 0:1024]
            d_y = p.dma("sp", yt, y_hbm[t * 128:(t + 1) * 128, :],
                        deps=prev_all)
            e4 = p.add("dve", lambda v, xt=xt, yt=yt:
                       v.tensor_add(xt, xt, yt), deps=[e3, d_y])
            prevst[par] = p.dma("sp", x_out[t * 128:(t + 1) * 128, :], xt,
                                deps=[e4])
        p.barrier()


B2_STAGE = "full"


def emit_block2(ctx, W, x_in, x_out, mk_scratch):
    p = ctx.p
    y0 = mk_scratch("b2_y0", [T, D], F32)
    y1 = mk_scratch("b2_y1", [T, D], F32)
    y2 = mk_scratch("b2_y2", [T, D], F32)
    y3 = mk_scratch("b2_y3", [T, D], F32)
    y4 = mk_scratch("b2_y4", [T, D], F32)
    xn_hbm = mk_scratch("b2_xnfm", [128, 8 * T], BF16)
    emit_ln(ctx, x_in, f32_dump=(x_out if B2_STAGE == "y0" else y0))
    if B2_STAGE == "y0":
        return
    for i in range(4):
        p.dma("sp", xn_hbm[:, i * 8192:(i + 1) * 8192],
              ctx.XNFM[:, i * 8192:(i + 1) * 8192])
    p.barrier()
    l2_hawk(ctx, W, y0, x_out if B2_STAGE == "hawk" else y1, mk_scratch)
    if B2_STAGE == "hawk":
        return
    if B2_STAGE == "rms1":
        emit_rms(ctx, y1, W["L2_attn_gn_hgmlp"])
        dump_fm(ctx, lambda c, t0, n: ctx.XNFM[:, fmcols(c, t0, n)], 8, x_out)
        return
    l2_gmlp(ctx, W, "L2_attn_g1_Wgrow", "L2_attn_g1_Wshrink",
            W["L2_attn_gn_hgmlp"], y1, x_out if B2_STAGE == "gmlp1" else y2)
    if B2_STAGE == "gmlp1":
        return
    l2_smqa(ctx, W, y2, x_out if B2_STAGE == "smqa" else y3)
    if B2_STAGE == "smqa":
        return
    l2_gmlp(ctx, W, "L2_attn_g2_Wgrow", "L2_attn_g2_Wshrink",
            W["L2_attn_gn_sgmlp"], y3, x_out if B2_STAGE == "gmlp2" else y4)
    if B2_STAGE == "gmlp2":
        return
    l2_ffn(ctx, W, xn_hbm, x_in, y4, x_out, mk_scratch)


B3_STAGE = "full"


def l3_attn(ctx, W, x_in, xb_hbm, mk):
    p = ctx.p
    p.barrier()
    z_hbm = mk("b3_z", [T, 2048], BF16)
    xbc_hbm = mk("b3_xbc", [128, 18 * T], BF16)
    xbcs_hbm = {d: mk(f"b3_xbcs_{d}", [128, 18 * T], BF16) for d in "fr"}
    cum_hbm = {d: mk(f"b3_cum_{d}", [32, T], F32) for d in "fr"}
    xa_hbm = xb_hbm if B3_STAGE == "attnf" else mk("b3_xa", [T, D], F32)
    NCH = 18

    twz, wzT = load_wT(ctx, W["L3_attn_W_in"][0:2048, :], ctx.FMB, 0)
    p.barrier()
    prevz = {0: None, 1: None}
    for t in range(NT):
        par = t % 2

        def fz(te, t=t):
            last = None
            for n in range(4):
                for d in range(8):
                    last = te.matmul(ctx.PS[n][:, 0:512],
                                     ctx.XNFM[:, fmcols(d, t * 128, 128)],
                                     wzT(d, n * 512, 512),
                                     start=(d == 0), stop=(d == 7))
            return last
        mmv = ctx.mm(fz, [0, 1, 2, 3])
        zt = ctx.SMB[:, par * 2048:par * 2048 + 2048]
        ev = [ctx.ev("dve", lambda v, n=n, zt=zt:
                     v.tensor_copy(zt[:, n * 512:(n + 1) * 512],
                                   ctx.PS[n][:, 0:512]),
                     [n], deps=[mmv, prevz[par]]) for n in range(4)]
        prevz[par] = p.dma("sp", z_hbm[t * 128:(t + 1) * 128, :], zt, deps=ev)
    p.barrier()
    twx, wxT = load_wT(ctx, W["L3_attn_W_in"][2048:4352, :], ctx.FMB, 0)
    p.barrier()
    prevg = {0: None, 1: None}
    for c in range(NCH):
        for g in range(8):
            par = (c * 8 + g) % 2
            bank = 4 + par

            def f(te, c=c, g=g, bank=bank):
                last = None
                for d in range(8):
                    last = te.matmul(ctx.PS[bank][:, 0:512], wxT(d, c * 128, 128),
                                     ctx.XNFM[:, fmcols(d, g * 512, 512)],
                                     start=(d == 0), stop=(d == 7))
                return last
            mmv = ctx.mm(f, [bank])
            stg = ctx.SMB[:, par * 512:par * 512 + 512]
            e1 = ctx.ev("dve", lambda v, stg=stg, bank=bank:
                        v.tensor_copy(stg, ctx.PS[bank][:, 0:512]),
                        [bank], deps=[mmv, prevg[par]])
            prevg[par] = p.dma("sp",
                               xbc_hbm[:, c * T + g * 512:c * T + (g + 1) * 512],
                               stg, deps=[e1])
    p.barrier()
    twd, wdT = load_wT(ctx, W["L3_attn_W_in"][4352:4384, :], ctx.WS, 0)
    p.barrier()
    for t in range(NT):
        bank = t % 2

        def fd(te, t=t, bank=bank):
            last = None
            for d in range(8):
                last = te.matmul(ctx.PS[bank][:, 0:32], ctx.XNFM[:, fmcols(d, t * 128, 128)],
                                 wdT(d, 0, 32), start=(d == 0), stop=(d == 7))
            return last
        mmv = ctx.mm(fd, [bank])
        ctx.ev("dve", lambda v, t=t, bank=bank:
               v.tensor_copy(ctx.MSKT[:, t * 32:(t + 1) * 32],
                             ctx.PS[bank][:, 0:32]),
               [bank], deps=[mmv])
    p.barrier()

    two, woT = load_wT(ctx, W["L3_attn_W_out"], ctx.FMB, 16384)
    p.barrier()
    for di, d_ in enumerate("f" if B3_STAGE == "attnf" else "fr"):
        sfx = "_" + d_
        if B3_STAGE == "sdump" and d_ == "r":
            dga = globals()["_dbg3_ap"]
            p.dma("pool", dga[0:128, :], xa_hbm[2000:2128, 0:224])
            p.barrier()
        rr = ctx.SCR[0:1, 0:32]
        ra = ctx.SCR[0:1, 32:64]
        rd = ctx.SCR[0:1, 64:96]
        d1 = p.dma("sp", rr, W["L3_attn_dt_bias" + sfx].unsqueeze(0))
        d2 = p.dma("sp", ra, W["L3_attn_A_log" + sfx].unsqueeze(0))
        d3 = p.dma("sp", rd, W["L3_attn_D" + sfx].unsqueeze(0))
        e1 = p.add("act", lambda s: s.activation(ra, ra, AF.Exp), deps=[d2])
        e2 = p.add("dve", lambda v: v.tensor_scalar_mul(ra, ra, -1.0), deps=[e1])
        BIASB = ctx.STAT[:, 64:96]
        AB = ctx.STAT[:, 96:128]
        DB = ctx.STAT[:, 128:160]
        for row, dstb, dep in ((rr, BIASB, d1), (ra, AB, e2), (rd, DB, d3)):
            mmv = ctx.mm(lambda te, row=row: te.matmul(
                ctx.PS[6][:, 0:32], ctx.c["ones1_f32"], row,
                start=True, stop=True), [6], deps=[dep])
            ctx.ev("dve", lambda v, dstb=dstb: v.tensor_copy(
                dstb, ctx.PS[6][:, 0:32]), [6], deps=[mmv])
        p.barrier()
        rnw = ctx.SCR[0:1, 0:1024]
        rnw2 = ctx.SCR[0:1, 1024:2048]
        dn1 = p.dma("sp", rnw, W["L3_attn_norm_w" + sfx][0:1024].unsqueeze(0))
        dn2 = p.dma("sp", rnw2, W["L3_attn_norm_w" + sfx][1024:2048].unsqueeze(0))
        for i, (row, dep) in enumerate(((rnw, dn1), (rnw2, dn2))):
            for n0 in (0, 512):
                mmv = ctx.mm(lambda te, row=row, n0=n0: te.matmul(
                    ctx.PS[6][:, 0:512], ctx.c["ones1_f32"], row[0:1, n0:n0 + 512],
                    start=True, stop=True), [6], deps=[dep])
                ctx.ev("dve", lambda v, i=i, n0=n0: v.tensor_copy(
                    ctx.FMB[:, i * 1024 + n0:i * 1024 + n0 + 512],
                    ctx.PS[6][:, 0:512]), [6], deps=[mmv])
        for c in range(NCH):
            p.dma("sp", ctx.STAT[:, 160 + c * 4:160 + (c + 1) * 4],
                  W["L3_attn_conv_w" + sfx][c * 128:(c + 1) * 128, :])
            p.dma("sp", ctx.STAT[:, 232 + c:233 + c],
                  W["L3_attn_conv_b" + sfx][c * 128:(c + 1) * 128].unsqueeze(1))
        p.barrier()

        zpad = p.add("dve", lambda v: v.memset(ctx.SMB[:, 4096:4100], 0.0))
        for c in range(NCH):
            hoff = 4 if d_ == "f" else 0
            d_h = p.dma("sp", ctx.SMB[:, hoff:hoff + 4096],
                        xbc_hbm[:, c * T:(c + 1) * T])
            zp = p.add("dve", lambda v, hoff=hoff:
                       v.memset(ctx.SMB[:, 0:4] if hoff == 4 else
                                ctx.SMB[:, 4096:4100], 0.0))
            p.barrier()
            prevs = {0: None, 1: None}
            for s in range(8):
                par = s % 2
                acc = ctx.SCR[:, par * 512:par * 512 + 512]
                base = s * 512 + (1 if d_ == "f" else 0)
                kk0 = 0 if d_ == "f" else 3
                d1_ = p.add("dve", lambda v, base=base, acc=acc, c=c, kk0=kk0:
                            v.tensor_scalar(acc, ctx.SMB[:, base:base + 512],
                                            ctx.STAT[:, 160 + c * 4 + kk0:
                                                     161 + c * 4 + kk0],
                                            None, ALU.mult),
                            deps=[prevs[par]])
                for k in range(1, 4):
                    kk = k if d_ == "f" else 3 - k
                    d1_ = p.add("dve", lambda v, base=base, k=k, acc=acc, c=c,
                                kk=kk:
                                v.scalar_tensor_tensor(
                                    acc, ctx.SMB[:, base + k:base + 512 + k],
                                    ctx.STAT[:, 160 + c * 4 + kk:
                                             161 + c * 4 + kk],
                                    acc, ALU.mult, ALU.add),
                                deps=[d1_])
                d2_ = p.add("dve", lambda v, acc=acc, c=c:
                            v.tensor_scalar_add(acc, acc,
                                                ctx.STAT[:, 232 + c:233 + c]),
                            deps=[d1_])
                stg = ctx.SMB[:, 4100 + par * 512:4100 + par * 512 + 512]
                d3_ = p.add("act", lambda s_, stg=stg, acc=acc:
                            s_.activation(stg, acc, AF.Silu), deps=[d2_])
                prevs[par] = p.dma(
                    "sp", xbcs_hbm[d_][:, c * T + s * 512:c * T + (s + 1) * 512],
                    stg, deps=[d3_])
            p.barrier()

        if B3_STAGE == "sdump" and d_ == "r":
            dga = globals()["_dbg3_ap"]
            p.dma("pool", dga[128:256, :], xa_hbm[2000:2128, 0:224])
            p.barrier()
        for ci in range(NT):
            dtp = ctx.STAT[:, 0:32]
            w1_ = ctx.STAT[:, 32:64]
            c1 = p.add("dve", lambda v, ci=ci, dtp=dtp:
                       v.tensor_add(dtp, ctx.MSKT[:, ci * 32:(ci + 1) * 32],
                                    BIASB))
            c2 = p.add("act", lambda s: s.activation(w1_, dtp, AF.Abs), deps=[c1])
            c3 = p.add("act", lambda s: s.activation(w1_, w1_, AF.Exp, scale=-1.0),
                       deps=[c2])
            c4 = p.add("act", lambda s: s.activation(w1_, w1_, AF.Ln, bias=1.0),
                       deps=[c3])
            c5 = p.add("dve", lambda v: v.tensor_scalar_max(dtp, dtp, 0.0),
                       deps=[c4])
            c6 = p.add("dve", lambda v: v.tensor_add(dtp, dtp, w1_), deps=[c5])
            c7 = p.add("dve", lambda v: v.tensor_mul(w1_, dtp, AB), deps=[c6])
            tr = ctx.mm(lambda te: te.transpose(ctx.PS[6][0:32, 0:128], w1_,
                                                ctx.c["identf_sb"]),
                        [6], deps=[c7])
            abf = ctx.SCR[:, 0:128]
            cp = ctx.ev("dve", lambda v, abf=abf:
                        v.tensor_copy(abf[0:32, :], ctx.PS[6][0:32, 0:128]),
                        [6], deps=[tr])
            lc = ctx.SCR[:, 128:256]
            z1 = p.add("dve", lambda v: v.memset(ctx.SCR[0:32, 256:384], 0.0))
            sc = p.add("dve", lambda v, lc=lc, abf=abf:
                       v.tensor_tensor_scan(lc[0:32, :], abf[0:32, :],
                                            ctx.SCR[0:32, 256:384], 0.0,
                                            ALU.add, ALU.add),
                       deps=[cp, z1])
            if d_ == "f":
                st = p.dma("sp", cum_hbm[d_][:, ci * 128:(ci + 1) * 128],
                           lc[0:32, :], deps=[sc])
            else:
                rc = ctx.SCR[:, 384:512]
                r1 = p.add("dve", lambda v, rc=rc, lc=lc, abf=abf:
                           v.tensor_sub(rc[0:32, :], abf[0:32, :], lc[0:32, :]),
                           deps=[sc])
                r2 = p.add("dve", lambda v, rc=rc, lc=lc:
                           v.tensor_tensor(rc[0:32, :],
                                           rc[0:32, :],
                                           lc[0:32, 127:128].broadcast_to([32, 128]),
                                           ALU.add),
                           deps=[r1])
                st = p.dma("sp", cum_hbm[d_][:, ci * 128:(ci + 1) * 128],
                           rc[0:32, :], deps=[r2])
            p.barrier()

        if B3_STAGE == "sdump" and d_ == "r":
            dga = globals()["_dbg3_ap"]
            p.dma("pool", dga[256:384, :], xa_hbm[2000:2128, 0:224])
            p.barrier()
            p.dma("pool", dga[384:512, :], xbcs_hbm["r"][:, 1000:1224])
            p.dma("pool", dga[128:256, :], xbcs_hbm["r"][:, 3500:3724])
            p.dma("pool", dga[256:384, :], xbcs_hbm["r"][:, 16 * T + 1000:16 * T + 1224])
            p.dma("pool", dga[384:512, :], xbcs_hbm["r"][:, 17 * T + 3500:17 * T + 3724])
            p.barrier()
        if B3_STAGE == "cum" and d_ == "r":
            p.dma("sp", xb_hbm[0:128, :].rearrange("a b -> (a b)")
                  .rearrange("(h l) -> h l", h=32), cum_hbm[d_])
            p.barrier()
            return
        srun_zero = p.add("dve", lambda v: v.memset(
            ctx.FMB[:, 10240:16384].bitcast(F32)[:, 0:2048], 0.0))
        sbf_zero = p.add("dve", lambda v: v.memset(ctx.FMB[:, 4096:6144], 0.0))
        p.barrier()
        chunks = list(range(NT)) if d_ == "f" else list(range(NT - 1, -1, -1))
        prevst = {0: None, 1: None}
        for ci in chunks:
            S_run = ctx.FMB[:, 10240:16384].bitcast(F32)[:, 0:2048]
            S_bf = ctx.FMB[:, 4096:6144]
            L_all = ctx.FMB[:, 6144:10240]
            d_xs = p.dma("sp", ctx.SMB[:, 0:2048].rearrange("p (c z) -> p c z", z=128),
                         xbcs_hbm[d_].rearrange("p (c t) -> p c t", c=18)[
                             :, 0:16, ci * 128:(ci + 1) * 128])
            d_B = p.dma("sp", ctx.FMB[:, 2048:2176],
                        xbcs_hbm[d_][:, 16 * T + ci * 128:16 * T + (ci + 1) * 128])
            d_C = p.dma("sp", ctx.FMB[:, 2176:2304],
                        xbcs_hbm[d_][:, 17 * T + ci * 128:17 * T + (ci + 1) * 128])
            FMCF = ctx.FMC.bitcast(F32)
            crow = FMCF[0:1, 0:4096]
            d_cr = p.dma("sp", crow.rearrange("o (h l) -> o h l", h=32),
                         cum_hbm[d_][:, ci * 128:(ci + 1) * 128].unsqueeze(0))
            cfm = FMCF[0:32, 4096:4224]
            d_cf = p.dma("sp", cfm,
                         cum_hbm[d_][:, ci * 128:(ci + 1) * 128])
            p.barrier()
            dtp = ctx.STAT[:, 0:32]
            w1_ = ctx.STAT[:, 32:64]
            c1 = p.add("dve", lambda v, ci=ci:
                       v.tensor_add(dtp, ctx.MSKT[:, ci * 32:(ci + 1) * 32], BIASB))
            c2 = p.add("act", lambda s: s.activation(w1_, dtp, AF.Abs), deps=[c1])
            c3 = p.add("act", lambda s: s.activation(w1_, w1_, AF.Exp, scale=-1.0),
                       deps=[c2])
            c4 = p.add("act", lambda s: s.activation(w1_, w1_, AF.Ln, bias=1.0),
                       deps=[c3])
            c5 = p.add("dve", lambda v: v.tensor_scalar_max(dtp, dtp, 0.0),
                       deps=[c4])
            c6 = p.add("dve", lambda v: v.tensor_add(dtp, dtp, w1_), deps=[c5])
            trc = ctx.mm(lambda te: te.transpose(ctx.PS[6][:, 0:32],
                                                 cfm,
                                                 ctx.c["identf_sb"][0:32, 0:32]),
                         [6], deps=[d_cf])
            cumt = ctx.STAT[:, 192:224]
            cpc = ctx.ev("dve", lambda v: v.tensor_copy(cumt, ctx.PS[6][:, 0:32]),
                         [6], deps=[trc])
            ept = ctx.STAT[:, 224:256]
            e_ep = p.add("act", lambda s: s.activation(ept, cumt, AF.Exp),
                         deps=[cpc])
            totcol = (cfm[0:32, 127:128] if d_ == "f"
                      else cfm[0:32, 0:1])
            trt = ctx.mm(lambda te, totcol=totcol: te.transpose(
                ctx.PS[6][0:1, 0:32], totcol, ctx.c["identf_sb"][0:32, 0:32]),
                [6], deps=[d_cf, cpc])
            totrow = FMCF[0:1, 4224:4256]
            cpt = ctx.ev("dve", lambda v: v.tensor_copy(totrow, ctx.PS[6][0:1, 0:32]),
                         [6], deps=[trt])
            mmt = ctx.mm(lambda te: te.matmul(ctx.PS[6][:, 0:32],
                                              ctx.c["ones1_f32"], totrow,
                                              start=True, stop=True),
                         [6], deps=[cpt])
            totb = ctx.STAT[:, 256:288]
            cpt2 = ctx.ev("dve", lambda v: v.tensor_copy(totb, ctx.PS[6][:, 0:32]),
                          [6], deps=[mmt])
            dst_ = ctx.STAT[:, 288:320]
            e_ds1 = p.add("dve", lambda v: v.tensor_sub(dst_, totb, cumt),
                          deps=[cpt2, cpc])
            e_ds = p.add("act", lambda s: s.activation(dst_, dst_, AF.Exp),
                         deps=[e_ds1])
            alb = ctx.STAT[:, 320:352]
            e_al = p.add("act", lambda s: s.activation(alb, totb, AF.Exp),
                         deps=[cpt2])

            lops = []
            for n in range(8):
                mmv = ctx.mm(lambda te, n=n: te.matmul(
                    ctx.PS[n][:, 0:512], ctx.c["ones1_f32"],
                    crow[0:1, n * 512:(n + 1) * 512],
                    start=True, stop=True), [n], deps=[d_cr])
                s1_ = ctx.ev("dve", lambda v, n=n: v.tensor_tensor(
                    ctx.PS[n].rearrange("p (h l) -> p h l", l=128),
                    ctx.PS[n].rearrange("p (h l) -> p h l", l=128),
                    cumt[:, n * 4:(n + 1) * 4].rearrange(
                        "p (h o) -> p h o", o=1).broadcast_to([128, 4, 128]),
                    ALU.subtract), [n], deps=[mmv, cpc])
                mconst = ctx.c["negl"] if d_ == "f" else ctx.c["negu"]
                s2_ = ctx.ev("dve", lambda v, n=n, mconst=mconst: v.tensor_tensor(
                    ctx.PS[n].rearrange("p (h l) -> p h l", l=128),
                    ctx.PS[n].rearrange("p (h l) -> p h l", l=128),
                    mconst.unsqueeze(1).broadcast_to([128, 4, 128]),
                    ALU.add), [n], deps=[s1_])
                lops.append(ctx.ev("act", lambda s, n=n: s.activation(
                    L_all[:, n * 512:(n + 1) * 512], ctx.PS[n][:, 0:512],
                    AF.Exp), [n], deps=[s2_]))
            mmg = ctx.mm(lambda te: te.matmul(ctx.PS[6][:, 0:128],
                                              ctx.FMB[:, 2048:2176],
                                              ctx.FMB[:, 2176:2304],
                                              start=True, stop=True),
                         [6], deps=[d_B, d_C])
            gtm = ctx.FMB[:, 2432:2560]
            e_gt = ctx.ev("dve", lambda v: v.tensor_copy(gtm, ctx.PS[6][:, 0:128]),
                          [6], deps=[mmg])
            e_lg = p.add("dve", lambda v: v.tensor_tensor(
                L_all.rearrange("p (h l) -> p h l", l=128),
                L_all.rearrange("p (h l) -> p h l", l=128),
                gtm.unsqueeze(1).broadcast_to([128, 32, 128]),
                ALU.mult), deps=lops + [e_gt])
            trb = ctx.mm(lambda te: te.transpose(ctx.PSB[6][:, 0:128],
                                                 ctx.FMB[:, 2048:2176],
                                                 ctx.c["ident_bf"]),
                         [6], deps=[d_B, e_gt])
            btm = ctx.FMB[:, 2304:2432]
            e_bt = ctx.ev("dve", lambda v: v.tensor_copy(btm, ctx.PSB[6][:, 0:128]),
                          [6], deps=[trb])
            xst = ctx.SMB[:, 0:2048]
            xstm = ctx.SMB[:, 2048:4096]
            lasttr = None
            for c0 in range(0, 16, 4):
                bank = 6 + (c0 // 4) % 2

                def ftr(te, c0=c0, bank=bank):
                    last = None
                    for j in range(4):
                        last = te.transpose(
                            ctx.PSB[bank][:, j * 128:(j + 1) * 128],
                            xst[:, (c0 + j) * 128:(c0 + j + 1) * 128],
                            ctx.c["ident_bf"])
                    return last
                tr = ctx.mm(ftr, [bank], deps=[d_xs, lasttr, e_bt])
                lasttr = ctx.ev("dve", lambda v, c0=c0, bank=bank:
                                v.tensor_copy(xstm[:, c0 * 128:(c0 + 4) * 128],
                                              ctx.PSB[bank][:, 0:512]),
                                [bank], deps=[tr])
            xd = ctx.SMB[:, 4096:6144]
            e_xd = p.add("dve", lambda v: v.tensor_tensor(
                xd.rearrange("p (h z) -> p h z", z=64),
                xstm.rearrange("p (h z) -> p h z", z=64),
                dtp.rearrange("p (h o) -> p h o", o=1).broadcast_to([128, 32, 64]),
                ALU.mult), deps=[lasttr, c6])
            def fyd(te):
                last = None
                for h in range(32):
                    last = te.matmul(ctx.PS[h // 8][:, (h % 8) * 64:(h % 8) * 64 + 64],
                                     L_all[:, h * 128:(h + 1) * 128],
                                     xd[:, h * 64:(h + 1) * 64],
                                     start=True, stop=True)
                return last
            m_yd = ctx.mm(fyd, [0, 1, 2, 3], deps=[e_lg, e_xd])
            Y = ctx.SCR[:, 0:2048]
            yev = []
            for n in range(4):
                yev.append(ctx.ev("dve", lambda v, n=n: v.tensor_tensor(
                    Y[:, n * 512:(n + 1) * 512].rearrange("p (h z) -> p h z", z=64),
                    ctx.PS[n].rearrange("p (h z) -> p h z", z=64),
                    ept[:, n * 8:(n + 1) * 8].rearrange(
                        "p (h o) -> p h o", o=1).broadcast_to([128, 8, 64]),
                    ALU.mult), [n], deps=[m_yd, e_ep]))
            def fyo(te):
                last = None
                for n in range(4):
                    last = te.matmul(ctx.PS[n][:, 0:512], ctx.FMB[:, 2176:2304],
                                     S_bf[:, n * 512:(n + 1) * 512],
                                     start=True, stop=True)
                return last
            m_yo = ctx.mm(fyo, [0, 1, 2, 3], deps=[d_C] + yev +
                          ([sbf_zero] if ci == chunks[0] else []))
            yev2 = []
            for n in range(4):
                s1_ = ctx.ev("dve", lambda v, n=n: v.tensor_tensor(
                    ctx.PS[n].rearrange("p (h z) -> p h z", z=64),
                    ctx.PS[n].rearrange("p (h z) -> p h z", z=64),
                    ept[:, n * 8:(n + 1) * 8].rearrange(
                        "p (h o) -> p h o", o=1).broadcast_to([128, 8, 64]),
                    ALU.mult), [n], deps=[m_yo, e_ep])
                yev2.append(ctx.ev("dve", lambda v, n=n: v.tensor_add(
                    Y[:, n * 512:(n + 1) * 512], Y[:, n * 512:(n + 1) * 512],
                    ctx.PS[n][:, 0:512]), [n], deps=[s1_]))
            e_xdd2 = p.add("dve", lambda v: v.tensor_tensor(
                xst.rearrange("p (h z) -> p h z", z=64),
                xd.rearrange("p (h z) -> p h z", z=64),
                dst_.rearrange("p (h o) -> p h o", o=1).broadcast_to([128, 32, 64]),
                ALU.mult), deps=[e_ds, e_xd])

            def fst(te):
                last = None
                for n in range(4):
                    last = te.matmul(ctx.PS[n][:, 0:512], btm,
                                     xst[:, n * 512:(n + 1) * 512],
                                     start=True, stop=True)
                return last
            m_st = ctx.mm(fst, [0, 1, 2, 3], deps=[e_bt, e_xdd2] + yev2)
            s_u1 = p.add("dve", lambda v: v.tensor_tensor(
                S_run.rearrange("p (h z) -> p h z", z=64),
                S_run.rearrange("p (h z) -> p h z", z=64),
                alb.rearrange("p (h o) -> p h o", o=1).broadcast_to([128, 32, 64]),
                ALU.mult), deps=[e_al, m_yo] +
                ([srun_zero] if ci == chunks[0] else []))
            s_u2 = []
            for n in range(4):
                s_u2.append(ctx.ev("dve", lambda v, n=n: v.tensor_add(
                    S_run[:, n * 512:(n + 1) * 512],
                    S_run[:, n * 512:(n + 1) * 512], ctx.PS[n][:, 0:512]),
                    [n], deps=[m_st, s_u1]))
            s_u3 = p.add("dve", lambda v: v.tensor_copy(S_bf, S_run), deps=s_u2)
            if B3_STAGE == "sdump" and d_ == "r":
                p.dma("sp", xb_hbm[ci * 128:(ci + 1) * 128, :].bitcast(BF16),
                      S_bf, deps=[s_u3])
            g1 = p.add("dve", lambda v: v.tensor_tensor(
                xstm.rearrange("p (h z) -> p h z", z=64),
                xstm.rearrange("p (h z) -> p h z", z=64),
                DB.rearrange("p (h o) -> p h o", o=1).broadcast_to([128, 32, 64]),
                ALU.mult), deps=yev2 + [e_xdd2])
            g2 = p.add("dve", lambda v: v.tensor_add(Y, Y, xstm), deps=[g1])
            zt = ctx.SMB[:, 4096:6144]
            d_z = p.dma("sp", zt, z_hbm[ci * 128:(ci + 1) * 128, :],
                        deps=[e_xdd2, g1, m_yd])
            g3 = p.add("act", lambda s, zt=zt: s.activation(zt, zt, AF.Silu),
                       deps=[d_z])
            g4 = p.add("dve", lambda v, zt=zt: v.tensor_mul(Y, Y, zt),
                       deps=[g2, g3])
            sqo = ctx.SMB[:, 0:2048]
            if B3_STAGE == "sdump" and d_ == "r":
                DQ = ctx.FMC[:, 8704:8928]
                cpd = p.add("dve", lambda v, DQ=DQ, zt=zt: [
                    v.tensor_copy(DQ[:, 0:96], Y[:, 0:96]),
                    v.tensor_copy(DQ[:, 96:160], Y[:, 1024:1088]),
                    v.tensor_copy(DQ[:, 160:224], zt[:, 0:64])][-1],
                    deps=[g4, g3])
                dga = globals()["_dbg3_ap"]
                pass
                g4b = cpd
            else:
                g4b = g4
            g5 = p.add("act", lambda s: s.activation(sqo, Y, AF.Square),
                       deps=[g4, g4b, m_st])
            sqc = ctx.STAT[:, 352:353]
            g6 = p.add("dve", lambda v: v.tensor_reduce(sqc, sqo, AX.X, ALU.add),
                       deps=[g5])
            g7 = p.add("act", lambda s: s.activation(sqc, sqc, AF.Ln,
                                                     scale=1.0 / 2048.0,
                                                     bias=1e-5), deps=[g6])
            g8 = p.add("act", lambda s: s.activation(sqc, sqc, AF.Exp, scale=-0.5),
                       deps=[g7])
            ybf = ctx.SMB[:, 2048:4096]
            g9 = p.add("dve", lambda v: v.scalar_tensor_tensor(
                ybf, Y, sqc, ctx.FMB[:, 0:2048], ALU.mult, ALU.mult),
                deps=[g8, g2, g4])
            yf = ctx.SMB[:, 0:2048]
            lasttr2 = None
            for c0 in range(0, 16, 4):
                bank = 6 + (c0 // 4) % 2

                def ftr2(te, c0=c0, bank=bank):
                    last = None
                    for j in range(4):
                        last = te.transpose(
                            ctx.PSB[bank][:, j * 128:(j + 1) * 128],
                            ybf[:, (c0 + j) * 128:(c0 + j + 1) * 128],
                            ctx.c["ident_bf"])
                    return last
                tr = ctx.mm(ftr2, [bank], deps=[g9, lasttr2])
                lasttr2 = ctx.ev("dve", lambda v, c0=c0, bank=bank:
                                 v.tensor_copy(yf[:, c0 * 128:(c0 + 4) * 128],
                                               ctx.PSB[bank][:, 0:512]),
                                 [bank], deps=[tr, g6])
            def fo(te):
                last = None
                for n in range(2):
                    for c in range(16):
                        last = te.matmul(ctx.PS[4 + n][:, 0:512],
                                         yf[:, c * 128:(c + 1) * 128],
                                         woT(c, n * 512, 512),
                                         start=(c == 0), stop=(c == 15))
                return last
            mmo = ctx.mm(fo, [4, 5], deps=[lasttr2])
            src_hbm = x_in if d_ == "f" else xa_hbm
            dst_hbm = xa_hbm if d_ == "f" else xb_hbm
            resid_evac(ctx, p, src_hbm, dst_hbm, ci, (4, 5), prevst,
                       extra=[g9])
            if B3_STAGE == "sdump" and d_ == "f" and ci == 10:
                evs, xtq = ctx._last_resid
                DQ2 = ctx.FMC[:, 8928:9376]
                cp2 = p.add("dve", lambda v, DQ2=DQ2, xtq=xtq: [
                    v.tensor_copy(DQ2[:, 0:96], ybf[:, 0:96]),
                    v.tensor_copy(DQ2[:, 96:224], xtq[:, 0:128])][-1],
                    deps=evs + [g9])
                p.dma("pool", globals()["_dbg3_ap"][512:640, :],
                      DQ2[:, 0:224], deps=[cp2])
            p.barrier()


def l3_ffn(ctx, W, xn_hbm, xb_hbm, x_out, mk):
    p = ctx.p
    p.barrier()
    gfm_hbm = mk("b3_gfm", [128, 24 * T], BF16)
    hfm_hbm = mk("b3_hfm", [128, 24 * T], BF16)
    for i in range(4):
        p.dma("sp", ctx.XNFM[:, i * 8192:(i + 1) * 8192],
              xn_hbm[:, i * 8192:(i + 1) * 8192])
    twg, wgT = load_wT(ctx, W["L3_ffn_w_gate"], ctx.FMB, 0, scale=1.0 / 127.0)
    p.barrier()
    prevg = {0: None, 1: None}
    for c in range(24):
        for g in range(8):
            par = (c * 8 + g) % 2
            bank = 4 + par

            def f(te, c=c, g=g, bank=bank):
                last = None
                for d in range(8):
                    last = te.matmul(ctx.PS[bank][:, 0:512], wgT(d, c * 128, 128),
                                     ctx.XNFM[:, fmcols(d, g * 512, 512)],
                                     start=(d == 0), stop=(d == 7))
                return last
            mmv = ctx.mm(f, [bank])
            stg = ctx.SMB[:, par * 512:par * 512 + 512]
            e1 = ctx.ev("dve", lambda v, stg=stg, bank=bank:
                        v.tensor_copy(stg, ctx.PS[bank][:, 0:512]),
                        [bank], deps=[mmv, prevg[par]])
            prevg[par] = p.dma("sp",
                               gfm_hbm[:, c * T + g * 512:c * T + (g + 1) * 512],
                               stg, deps=[e1])
    p.barrier()
    twu, wuT = load_wT(ctx, W["L3_ffn_w_up"], ctx.FMB, 0, scale=1.0 / 127.0)
    p.barrier()
    prevh = {0: None, 1: None}
    for c in range(24):
        for g in range(8):
            par = (c * 8 + g) % 2
            bank = 4 + par

            def f(te, c=c, g=g, bank=bank):
                last = None
                for d in range(8):
                    last = te.matmul(ctx.PS[bank][:, 0:512], wuT(d, c * 128, 128),
                                     ctx.XNFM[:, fmcols(d, g * 512, 512)],
                                     start=(d == 0), stop=(d == 7))
                return last
            mmv = ctx.mm(f, [bank])
            gt = ctx.SMB[:, 2048 + par * 512:2048 + par * 512 + 512]
            d_g = p.dma("sp", gt, gfm_hbm[:, c * T + g * 512:c * T + (g + 1) * 512],
                        deps=[prevh[par]], nobarrier=True)
            sg = ctx.SCR[:, par * 512:par * 512 + 512]
            e1 = p.add("act", lambda s, sg=sg, gt=gt:
                       s.activation(sg, gt, AF.Silu), deps=[d_g])
            stg = ctx.SMB[:, par * 512:par * 512 + 512]
            e2 = ctx.ev("dve", lambda v, stg=stg, sg=sg, bank=bank:
                        v.tensor_mul(stg, sg, ctx.PS[bank][:, 0:512]),
                        [bank], deps=[mmv, e1, prevh[par]])
            prevh[par] = p.dma("sp",
                               hfm_hbm[:, c * T + g * 512:c * T + (g + 1) * 512],
                               stg, deps=[e2])
    p.barrier()
    twd, wdT = load_wT(ctx, W["L3_ffn_w_down"], ctx.FMB, 0, scale=1.0 / 127.0)
    p.barrier()
    prevst = {0: None, 1: None}
    for st_ in range(8):
        d_h = p.dma("sp", ctx.FMC[:, 0:12288].rearrange("p (c z) -> p c z", z=512),
                    hfm_hbm.rearrange("p (c t) -> p c t", c=24)[
                        :, :, st_ * 512:(st_ + 1) * 512])
        p.barrier()
        for tl in range(4):
            t = st_ * 4 + tl

            def f(te, tl=tl):
                last = None
                for n in range(2):
                    for c in range(24):
                        last = te.matmul(ctx.PS[4 + n][:, 0:512],
                                         ctx.FMC[:, c * 512 + tl * 128:
                                                 c * 512 + tl * 128 + 128],
                                         wdT(c, n * 512, 512),
                                         start=(c == 0), stop=(c == 23))
                return last
            mmv = ctx.mm(f, [4, 5], deps=[d_h])
            resid_evac(ctx, p, xb_hbm, x_out, t, (4, 5), prevst)
    p.barrier()


def emit_block3(ctx, W, x_in, x_out, mk_scratch):
    p = ctx.p
    mk_scratch("b3_xa", [T, D], F32)
    xn_hbm = mk_scratch("b3_xnfm", [128, 8 * T], BF16)
    emit_ln(ctx, x_in)
    for i in range(4):
        p.dma("sp", xn_hbm[:, i * 8192:(i + 1) * 8192],
              ctx.XNFM[:, i * 8192:(i + 1) * 8192])
    p.barrier()
    xb = x_out if B3_STAGE in ("attnf", "attn", "cum", "sdump") else mk_scratch(
        "b3_xb", [T, D], F32)
    l3_attn(ctx, W, x_in, xb, mk_scratch)
    if B3_STAGE in ("attnf", "attn", "cum", "sdump"):
        return
    l3_ffn(ctx, W, xn_hbm, xb, x_out, mk_scratch)


CONST_VALS = [1e-5, 1e-6, 1e-8, 0.5, 2.0]


def make_consts(blocks):
    c = {}
    c["cvals"] = np.tile(np.asarray([CONST_VALS], dtype=np.float32), (128, 1))
    c["ident_bf"] = np.eye(128, dtype=BF)
    c["ones1_f32"] = np.ones((1, 128), dtype=np.float32)
    c["ones1_bf"] = np.ones((1, 128), dtype=BF)
    c["iota16_bc"] = np.tile(np.arange(16, dtype=np.float32), (128, 1))
    c["identf_sb"] = np.eye(128, dtype=np.float32)
    c["crow8x32"] = np.tile(8.0 - np.arange(8, dtype=np.float32), (128, 32))
    if 2 in blocks or 3 in blocks:
        R = np.zeros((128, 128), dtype=np.float32)
        for i in range(64):
            R[2 * i + 1, 2 * i] = -1.0
            R[2 * i, 2 * i + 1] = 1.0
        c["rotm"] = R.astype(BF)
        iu = np.triu(np.full((128, 128), NEG, dtype=np.float32), 1)
        c["negu"] = iu
        c["negl"] = np.tril(np.full((128, 128), NEG, dtype=np.float32), -1)
        invf = 1.0 / 10000.0 ** (np.arange(0, 128, 2, dtype=np.float64) / 128.0)
        tpos = np.arange(T, dtype=np.float64)
        fr = np.repeat(tpos[:, None] * invf[None, :], 2, axis=1)
        c["rot_cos"] = np.cos(fr).T.astype(BF).copy()
        c["rot_sin"] = np.sin(fr).T.astype(BF).copy()
    return c


BLOCK_WEIGHTS = {
    0: ["L0_attn_Wq", "L0_attn_Wk", "L0_attn_Wv", "L0_attn_proj"] +
       [f"L0_ffn_{p}_{s}" for p in "qkv"
        for s in ("node_w", "node_b", "w1", "b1", "w2", "b2")],
}
BLOCK_WEIGHTS[1] = ["L1_attn_W", "L1_attn_b", "L1_ffn_W1", "L1_ffn_b1",
                    "L1_ffn_W2", "L1_ffn_b2", "L1_ffn_expert_W",
                    "L1_ffn_expert_b"]
BLOCK_WEIGHTS[2] = (["L2_attn_gn_hawk", "L2_attn_gn_hgmlp", "L2_attn_gn_smqa",
                     "L2_attn_gn_sgmlp", "L2_attn_hawk_Win", "L2_attn_hawk_cw",
                     "L2_attn_hawk_cb", "L2_attn_hawk_Wg", "L2_attn_hawk_bg",
                     "L2_attn_hawk_fb", "L2_attn_hawk_Wout", "L2_attn_g1_Wgrow",
                     "L2_attn_g1_Wshrink", "L2_attn_g2_Wgrow",
                     "L2_attn_g2_Wshrink", "L2_attn_smqa_Wq",
                     "L2_attn_smqa_Wkv", "L2_attn_smqa_Wout", "L2_ffn_W_r",
                     "L2_ffn_W_k", "L2_ffn_W_v", "L2_ffn_mu_r", "L2_ffn_mu_k"])
BLOCK_WEIGHTS[3] = (["L3_attn_W_in", "L3_attn_W_out"] +
                    [f"L3_attn_{n}_{d}" for d in "fr"
                     for n in ("conv_w", "conv_b", "dt_bias", "A_log", "D",
                               "norm_w")] +
                    ["L3_ffn_w_gate", "L3_ffn_w_up", "L3_ffn_w_down"])
EMITTERS = {0: emit_block0, 1: emit_block1, 2: emit_block2, 3: emit_block3}


def build_group(blocks, wshapes):
    nc = bass.Bass("TRN2", target_bir_lowering=False, debug=False)
    x_in = nc.dram_tensor("x_cur", [T, D], F32, kind="ExternalInput").ap()
    x_out = nc.dram_tensor("x_next", [T, D], F32, kind="ExternalOutput").ap()
    W = {}
    for b in blocks:
        for name in BLOCK_WEIGHTS[b]:
            shape, dt = wshapes[name]
            W[name] = nc.dram_tensor(name, list(shape),
                                     I32 if dt == np.int32 else F32,
                                     kind="ExternalInput").ap()
    consts = make_consts(blocks)
    ctx = Ctx(nc, consts)
    scr = {}

    def mk_scratch(name, shape, dt):
        if name not in scr:
            scr[name] = nc.dram_tensor(name, list(shape), dt,
                                       kind="Internal").ap()
        return scr[name]

    if DBG_STAT:
        globals()["_dbg_ap"] = nc.dram_tensor("dbg_stat", [T, 4], F32,
                                              kind="ExternalOutput").ap()
    if B3_STAGE == "sdump":
        globals()["_dbg3_ap"] = nc.dram_tensor("dbg_b3", [T, 224], F32,
                                               kind="ExternalOutput").ap()
    cur = x_in
    for i, b in enumerate(blocks):
        nxt = x_out if i == len(blocks) - 1 else mk_scratch(
            f"x_mid{i}", [T, D], F32)
        EMITTERS[b](ctx, W, cur, nxt, mk_scratch)
        cur = nxt
    ctx.p.finalize()
    return nc, consts


def kernel(**inputs):
    x = np.ascontiguousarray(inputs["L0_attn_x"], dtype=np.float32)
    B = x.shape[0]
    wshapes = {k: (v.shape, v.dtype) for k, v in inputs.items()}
    for group in LAUNCH_PLAN:
        nc, consts = build_group(group, wshapes)
        in_maps = []
        for b in range(B):
            m = {"x_cur": np.ascontiguousarray(x[b])}
            for blk in group:
                for name in BLOCK_WEIGHTS[blk]:
                    m[name] = inputs[name]
            for name, arr in consts.items():
                m["c_" + name] = arr
            in_maps.append(m)
        res = run_bass_kernel_spmd(nc, in_maps, core_ids=list(range(B)))
        x = np.stack([r["x_next"] for r in res.results])
    return x
